# version 05 (final)
# baseline (speedup 1.0000x reference)
import contextlib
import numpy as np

import concourse.bass as bass
import concourse.mybir as mybir
from concourse.bass_utils import run_bass_kernel_spmd
from concourse.aot_env import allow_ici_all_to_all

F32 = mybir.dt.float32
AF = mybir.ActivationFunctionType
ALU = mybir.AluOpType

B, T, HID = 4, 8192, 2048
NKH_L, NVH_L = 8, 16
DK = DV = 128
C = 128
TT1 = 256
NT1 = T // TT1
TT2 = 512
TOW = T // 2
NT2 = TOW // TT2
QKVZ_L = NKH_L * 768
INTER = 8192
EPS_L2 = 1e-6
EPS_RMS = 1e-6
EPS_LN = 1e-5
NBIG = -1e30


class Tracker:

    def __init__(self, nc, sems, needed, record):
        self.nc = nc
        self.sems = sems
        self.needed = needed
        self.record = record
        self.cnt = {e: 0 for e in ("t", "v", "a", "s", "p")}
        self.dcnt = {}
        self.waited = {}
        self.last_w = {}
        self.readers = {}
        self.incmap = {}
        if not record:
            for e, s in self.needed.items():
                self.incmap[e] = {c: i + 1 for i, c in enumerate(sorted(s))}

    def eng_obj(self, e):
        nc = self.nc
        return {"t": nc.tensor, "v": nc.vector, "a": nc.scalar,
                "s": nc.sync, "p": nc.gpsimd}[e]

    def _wait(self, consumer, prod_eng, prod_cnt):
        if prod_cnt <= 0:
            return
        if prod_eng == consumer and prod_eng in ("t", "s"):
            return
        key = (consumer, prod_eng)
        if self.waited.get(key, 0) >= prod_cnt:
            return
        self.waited[key] = prod_cnt
        if prod_eng.startswith("dma:"):
            if not self.record:
                self.eng_obj(consumer).wait_ge(self.sems[prod_eng],
                                               prod_cnt * 16)
            return
        if self.record:
            self.needed[prod_eng].add(prod_cnt)
            return
        val = self.incmap[prod_eng][prod_cnt]
        self.eng_obj(consumer).wait_ge(self.sems[prod_eng], val)

    def dep(self, consumer, reads=(), writes=()):
        for k in reads:
            lw = self.last_w.get(k)
            if lw:
                self._wait(consumer, lw[0], lw[1])
        for k in writes:
            lw = self.last_w.get(k)
            if lw:
                self._wait(consumer, lw[0], lw[1])
            for (re, rc) in self.readers.get(k, ()):
                self._wait(consumer, re, rc)

    def done(self, ce, instr, reads=(), writes=()):
        if ce.startswith("dma:"):
            self.dcnt[ce] = c = self.dcnt.get(ce, 0) + 1
            if not self.record and instr is not None:
                instr.then_inc(self.sems[ce], 16)
        else:
            self.cnt[ce] += 1
            c = self.cnt[ce]
            if not self.record and instr is not None and c in self.needed[ce]:
                instr.then_inc(self.sems[ce], 1)
        for k in reads:
            self.readers.setdefault(k, []).append((ce, c))
        for k in writes:
            self.last_w[k] = (ce, c)
            self.readers[k] = []
        return c

    def mm(self, out_ap, lhsT, rhs, rk, wk, start=True, stop=True):
        rk = tuple(rk)
        self.dep("t", reads=rk, writes=(wk,) if start else ())
        i = None
        if not self.record:
            i = self.nc.tensor.matmul(out_ap, lhsT, rhs, start=start, stop=stop)
        self.done("t", i, reads=rk, writes=(wk,) if stop else ())

    def transpose(self, out_ap, in_ap, ident, rk, wk):
        rk = tuple(rk)
        self.dep("t", reads=rk, writes=(wk,))
        i = None if self.record else self.nc.tensor.transpose(out_ap, in_ap, ident)
        self.done("t", i, reads=rk, writes=(wk,))

    def act(self, out_ap, in_ap, func, rk, wk, accum_out=None, **kw):
        rk, wk = tuple(rk), tuple(wk)
        self.dep("a", reads=rk, writes=wk)
        i = None
        if not self.record:
            i = self.nc.scalar.activation(out_ap, in_ap, func,
                                          accum_out=accum_out, **kw)
        self.done("a", i, reads=rk, writes=wk)

    def vtt(self, out_ap, a_ap, b_ap, op, rk, wk, eng="v"):
        rk, wk = tuple(rk), tuple(wk)
        self.dep(eng, reads=rk, writes=wk)
        i = None if self.record else self.eng_obj(eng).tensor_tensor(
            out_ap, a_ap, b_ap, op)
        self.done(eng, i, reads=rk, writes=wk)

    def vts(self, out_ap, in_ap, s1, s2, op0, rk, wk, op1=None, eng="v",
            accum_out=None):
        rk, wk = tuple(rk), tuple(wk)
        self.dep(eng, reads=rk, writes=wk)
        i = None
        if not self.record:
            if op1 is None:
                i = self.eng_obj(eng).tensor_scalar(out_ap, in_ap, s1, s2, op0)
            else:
                i = self.eng_obj(eng).tensor_scalar(out_ap, in_ap, s1, s2,
                                                    op0, op1,
                                                    accum_out=accum_out)
        self.done(eng, i, reads=rk, writes=wk)

    def vcopy(self, out_ap, in_ap, rk, wk, eng="v"):
        rk, wk = tuple(rk), tuple(wk)
        self.dep(eng, reads=rk, writes=wk)
        i = None if self.record else self.eng_obj(eng).tensor_copy(out_ap, in_ap)
        self.done(eng, i, reads=rk, writes=wk)

    def vmax(self, out_ap, in_ap, rk, wk):
        rk, wk = tuple(rk), tuple(wk)
        self.dep("v", reads=rk, writes=wk)
        i = None if self.record else self.nc.vector.max(out_ap, in_ap)
        self.done("v", i, reads=rk, writes=wk)

    def memset(self, ap, val, wk, eng="v"):
        wk = tuple(wk)
        self.dep(eng, writes=wk)
        i = None if self.record else self.eng_obj(eng).memset(ap, val)
        self.done(eng, i, writes=wk)

    def load(self, out_ap, in_ap, wk, ch):
        wk = tuple(wk)
        self.dep("s", writes=wk)
        i = None
        if not self.record:
            i = self.nc.sync.dma_start(out=out_ap, in_=in_ap)
        self.done("dma:" + ch, i, writes=wk)

    def store(self, out_ap, in_ap, rk, ch):
        rk = tuple(rk)
        self.dep("s", reads=rk)
        i = None
        if not self.record:
            i = self.nc.sync.dma_start(out=out_ap, in_=in_ap)
        self.done("dma:" + ch, i, reads=rk)

    def fence_channel(self, ch):
        tot = self.dcnt.get("dma:" + ch, 0)
        for k, (e, c) in list(self.last_w.items()):
            if e == "dma:" + ch:
                self.last_w[k] = (e, tot)

    def raw(self, fn, *a, **k):
        if not self.record:
            fn(*a, **k)

    def wait_all_dma(self):
        if not self.record:
            for ch, n in sorted(self.dcnt.items()):
                self.nc.sync.wait_ge(self.sems[ch], 16 * n)


def host_constants():
    idx = np.arange(C)
    consts = {
        "ident": np.eye(C, dtype=np.float32),
        "tri_incl": (idx[:, None] <= idx[None, :]).astype(np.float32),
        "tri_rev": (idx[:, None] > idx[None, :]).astype(np.float32),
        "maskn_s": np.where(idx[:, None] < idx[None, :], 0.0, NBIG).astype(np.float32),
        "maskn_sT": np.where(idx[:, None] > idx[None, :], 0.0, NBIG).astype(np.float32),
        "maskn_i": np.where(idx[:, None] <= idx[None, :], 0.0, NBIG).astype(np.float32),
        "ones_col": np.ones((C, 1), np.float32),
        "ones_row": np.ones((1, C), np.float32),
        "qscale_row": np.full((1, C), 128.0 ** -0.5, np.float32),
    }
    eh = np.zeros((16, 16 * C), np.float32)
    for h in range(16):
        eh[h, h * C:(h + 1) * C] = 1.0
    consts["eh"] = eh
    return consts


def build_program(nt1=NT1, nt2=NT2, taps=False):
    nc = bass.Bass("TRN2", target_bir_lowering=False, debug=False)

    d_xfm = nc.dram_tensor("x_fm", [HID, T], F32, kind="ExternalInput")
    d_xown = nc.dram_tensor("x_own_fm", [HID, TOW], F32, kind="ExternalInput")
    d_wqkvz = nc.dram_tensor("wqkvz", [HID, QKVZ_L], F32, kind="ExternalInput")
    d_wb = nc.dram_tensor("wb", [HID, 16], F32, kind="ExternalInput")
    d_wa = nc.dram_tensor("wa", [HID, 16], F32, kind="ExternalInput")
    d_convw = nc.dram_tensor("convw", [C, 128], F32, kind="ExternalInput")
    d_dtb = nc.dram_tensor("dtb", [16, 1], F32, kind="ExternalInput")
    d_alog = nc.dram_tensor("alog", [16, 1], F32, kind="ExternalInput")
    d_nw = nc.dram_tensor("nw", [1, 128], F32, kind="ExternalInput")
    d_wout = nc.dram_tensor("wout", [2 * HID, HID], F32, kind="ExternalInput")
    d_gate = nc.dram_tensor("gate", [HID, 8], F32, kind="ExternalInput")
    d_up = nc.dram_tensor("up", [HID, INTER], F32, kind="ExternalInput")
    d_down = nc.dram_tensor("down", [INTER, HID], F32, kind="ExternalInput")
    d_expert = nc.dram_tensor("expert", [HID, HID], F32, kind="ExternalInput")
    d_gam = nc.dram_tensor("gam", [C, 16], F32, kind="ExternalInput")
    d_bet = nc.dram_tensor("bet", [C, 16], F32, kind="ExternalInput")
    d_tb = nc.dram_tensor("tokbase", [1, 1], mybir.dt.int32, kind="ExternalInput")
    hc = host_constants()
    dc = {k: nc.dram_tensor("c_" + k, list(v.shape), F32, kind="ExternalInput")
          for k, v in hc.items()}

    d_out = nc.dram_tensor("out_fm", [HID, TOW], F32, kind="ExternalOutput")
    d_oga = nc.dram_tensor("og_a2a", [HID, T], F32)
    d_ogf = nc.dram_tensor("og_full", [2 * HID, T], F32)
    d_tap = None
    if taps:
        d_tap = nc.dram_tensor("tap", [2 * HID, T], F32, kind="ExternalOutput")

    es = contextlib.ExitStack()
    with es:
        def sb(name, p, f):
            return es.enter_context(nc.sbuf_tensor(name, [p, f], F32))

        cs = {k: sb("cs_" + k, *hc[k].shape) for k in hc}
        t_wb = sb("t_wb", 128, 16 * 16)
        t_wa = sb("t_wa", 128, 16 * 16)
        t_convw = sb("t_convw", C, 128)
        t_dtb = sb("t_dtb", 16, 1)
        t_negea = sb("t_negea", 16, 1)
        t_nwrow = sb("t_nwrow", 1, 128)
        t_nwb = sb("t_nwb", C, 128)
        t_gam = sb("t_gam", C, 16)
        t_bet = sb("t_bet", C, 16)
        t_state = sb("t_state", C, NVH_L * DV)
        t_tail = sb("t_tail", C, 32 * 4)
        t_tb = es.enter_context(nc.sbuf_tensor("t_tb", [1, 1], mybir.dt.int32))
        ARENA_F = 46432
        arena = es.enter_context(nc.sbuf_tensor("arena", [128, ARENA_F], F32))

        class Carver:
            def __init__(self):
                self.off = 0

            def take(self, p, f):
                o = (self.off + 15) // 16 * 16
                self.off = o + f
                assert self.off <= ARENA_F, f"arena overflow {self.off}"
                return arena[0:p, o:o + f]

        c1 = Carver()
        t_xT = c1.take(C, 16 * TT1)
        t_wsl = c1.take(C, 3 * 1024)
        t_mix = c1.take(C, 32 * (TT1 + 4))
        t_qkv = c1.take(C, 32 * TT1)
        t_z = c1.take(C, 16 * TT1)
        t_cv = [c1.take(C, TT1) for i in range(4)]
        t_ba = c1.take(16, 2 * TT1)
        t_beta = c1.take(16, TT1)
        t_g = c1.take(16, TT1)
        t_sp = c1.take(16, TT1)
        t_lb = c1.take(16, TT1)
        t_cum = c1.take(C, 16)
        t_cumT = c1.take(16, C)
        t_lba = c1.take(C, 16)
        t_lbaT = c1.take(16, C)
        t_dect = c1.take(C, 16)
        t_bacol = c1.take(C, 16)
        t_btok = c1.take(C, 16)
        t_gtok = c1.take(C, 16)
        t_kh = c1.take(C, NKH_L * C)
        t_qh = c1.take(C, NKH_L * C)
        t_sq = c1.take(C, C)
        t_rstd = c1.take(1, 2 * C)
        RV = 4
        t_DB = [c1.take(C, C) for i in range(RV)]
        t_Dq = [c1.take(C, C) for i in range(RV)]
        t_DBz = [c1.take(C, C) for i in range(RV)]
        t_bA = [c1.take(C, C) for i in range(RV)]
        t_Y = [[c1.take(C, C) for j in range(2)] for i in range(RV)]
        t_Z = [[c1.take(C, C) for j in range(2)] for i in range(RV)]
        t_U = [[c1.take(C, C) for j in range(2)] for i in range(RV)]
        t_P = [c1.take(C, C) for i in range(RV)]
        t_Qa = [c1.take(C, C) for i in range(RV)]
        t_G0 = [c1.take(C, C) for i in range(2)]
        t_G1 = [c1.take(C, C) for i in range(2)]
        t_Kt = [c1.take(C, C) for i in range(2)]
        t_r1 = [c1.take(C, C) for i in range(RV)]
        t_r2 = [c1.take(C, C) for i in range(RV)]
        t_sz = [c1.take(C, C) for i in range(RV)]
        t_og0 = [c1.take(C, C) for i in range(RV)]
        t_msq0 = c1.take(C, C)
        t_ms = [c1.take(C, 1) for i in range(RV)]
        t_rst = [c1.take(C, 1) for i in range(RV)]
        t_ogt = c1.take(C, 16 * TT1)
        c2v = Carver()
        t_x1 = c2v.take(C, 16 * TT2)
        t_y2 = c2v.take(C, 16 * TT2)
        t_h2 = c2v.take(C, 8 * TT2)
        t_og2 = c2v.take(C, 2 * 8 * TT2)
        t_w2 = c2v.take(C, 4 * 1024)
        t_xr = c2v.take(C, 8 * TT2)
        t_sqa = c2v.take(C, 2 * TT2)
        t_st1 = c2v.take(1, TT2)
        t_st2 = c2v.take(1, TT2)
        t_st3 = c2v.take(1, TT2)
        t_lg = c2v.take(8, TT2)
        t_ltok = [c2v.take(C, 8) for i in range(4)]
        t_msk = [c2v.take(C, 8) for i in range(4)]
        t_m1 = [c2v.take(C, 1) for i in range(4)]
        t_m2c = [c2v.take(C, 1) for i in range(4)]
        t_cnt = [c2v.take(C, 1) for i in range(4)]
        t_d1 = [c2v.take(C, 1) for i in range(4)]
        t_t2r = c2v.take(1, TT2)
        t_t2b = c2v.take(C, TT2)
        t_o2 = c2v.take(C, 4 * TT2)

        pb = [es.enter_context(nc.psum_tensor(f"pb{i}", [C, 512], F32))
              for i in range(8)]
        PS = [("PS", i) for i in range(8)]

        sems = {}
        for e in ("t", "v", "a", "s", "p", "cc"):
            sems[e] = es.enter_context(nc.semaphore("sem_" + e))
        DMACH = ("pro", "xT", "wsl0", "wsl1", "wsl2", "wsl3", "ogst",
                 "w20", "w21", "w22", "w23", "og20", "og21", "xr", "outst")
        for ch in DMACH:
            sems["dma:" + ch] = es.enter_context(nc.semaphore("semd_" + ch))

        def prologue(tr):
            for k in dc:
                tr.load(cs[k][:], dc[k][:, :], wk=[("c", k)], ch="pro")
            tr.load(t_convw[:], d_convw[:, :], wk=[("convw",)], ch="pro")
            tr.load(t_dtb[:], d_dtb[:, :], wk=[("dtb",)], ch="pro")
            tr.load(t_negea[:], d_alog[:, :], wk=[("negea",)], ch="pro")
            tr.load(t_gam[:], d_gam[:, :], wk=[("gam",)], ch="pro")
            tr.load(t_bet[:], d_bet[:, :], wk=[("bet",)], ch="pro")
            for kc in range(16):
                tr.load(t_wb[:, kc * 16:(kc + 1) * 16],
                        d_wb[kc * 128:(kc + 1) * 128, :], wk=[("wbk", kc)],
                        ch="pro")
                tr.load(t_wa[:, kc * 16:(kc + 1) * 16],
                        d_wa[kc * 128:(kc + 1) * 128, :], wk=[("wak", kc)],
                        ch="pro")
            tr.load(t_nwrow[:], d_nw[:, :], wk=[("nwrow",)], ch="pro")
            tr.load(t_tb[:], d_tb[:, :], wk=[("tb",)], ch="pro")
            tr.fence_channel("pro")
            tr.mm(pb[0][:, 0:128], cs["ones_row"][:], t_nwrow[:],
                  rk=[("c", "ones_row"), ("nwrow",)], wk=PS[0])
            tr.act(t_nwb[:], pb[0][:, 0:128], AF.Copy, rk=[PS[0]], wk=[("nwb",)])
            tr.act(t_negea[:], t_negea[:], AF.Exp, rk=[("negea",)], wk=[("negea",)])
            tr.vts(t_negea[:], t_negea[:], -1.0, None, ALU.mult,
                   rk=[("negea",)], wk=[("negea",)])
            tr.memset(t_state[:], 0.0, wk=[("sv", v) for v in range(16)])
            tr.memset(t_tail[:], 0.0, wk=[("tail", cc_) for cc_ in range(32)])

        def p1_body(tr, it):
            tr.it_regs = it["regs"]
            ogrow = ogcol = 0
            rx_ = it["regs"]["rx"]
            rtmp_ = it["regs"]["tmp"]
            for half in range(2):
                if half == 0:
                    off_reg = rx_
                else:
                    tr.raw(nc.sync.reg_add, rtmp_, rx_, 8 * 128 * T)
                    off_reg = rtmp_
                ap = bass.AP(d_xfm, off_reg,
                             [[T, 128], [128 * T, 8], [1, TT1]])
                tr.load(t_xT[:, half * 8 * TT1:(half + 1) * 8 * TT1], ap,
                        wk=[("xT", fc) for fc in
                            range(half * 8, half * 8 + 8)], ch="xT")
            tr.fence_channel("xT")
            for grp in range(6):
                for kc in range(16):
                    wb_ = kc % 3
                    slab = t_wsl[:, wb_ * 1024:wb_ * 1024 + 1024]
                    tr.load(slab,
                            d_wqkvz[kc * 128:(kc + 1) * 128,
                                    grp * 1024:(grp + 1) * 1024],
                            wk=[("wsl", wb_)], ch=f"wsl{wb_}")
                    for m in range(8):
                        tr.mm(pb[m][:, 0:TT1],
                              slab[:, m * 128:(m + 1) * 128],
                              t_xT[:, kc * TT1:(kc + 1) * TT1],
                              rk=[("wsl", wb_), ("xT", kc)],
                              wk=PS[m], start=(kc == 0), stop=(kc == 15))
                for m in range(8):
                    mc = grp * 8 + m
                    kh, r = divmod(mc, 6)
                    if r == 4 or r == 5:
                        zc = 2 * kh + (r - 4)
                        tr.act(t_z[:, zc * TT1:(zc + 1) * TT1], pb[m][:, 0:TT1],
                               AF.Copy, rk=[PS[m]], wk=[("z", zc)])
                        continue
                    if r == 0:
                        cc_ = kh
                    elif r == 1:
                        cc_ = 8 + kh
                    else:
                        cc_ = 16 + 2 * kh + (r - 2)
                    base = cc_ * (TT1 + 4)
                    tr.vcopy(t_mix[:, base:base + 3],
                             t_tail[:, cc_ * 4:cc_ * 4 + 3],
                             rk=[("tail", cc_)], wk=[("mixh", cc_)])
                    tr.act(t_mix[:, base + 3:base + 3 + TT1], pb[m][:, 0:TT1],
                           AF.Copy, rk=[PS[m]], wk=[("mix", cc_)])
                    tr.vcopy(t_tail[:, cc_ * 4:cc_ * 4 + 3],
                             t_mix[:, base + TT1:base + TT1 + 3],
                             rk=[("mix", cc_)], wk=[("tail", cc_)])
            for outc, wsb, keyn in ((0, t_wb, "wbk"), (1, t_wa, "wak")):
                for kc in range(16):
                    tr.mm(pb[0][0:16, 256 * outc:256 * outc + TT1],
                          wsb[:, kc * 16:(kc + 1) * 16],
                          t_xT[:, kc * TT1:(kc + 1) * TT1],
                          rk=[(keyn, kc), ("xT", kc)],
                          wk=PS[0], start=(kc == 0), stop=(kc == 15))
                tr.act(t_ba[:, outc * TT1:(outc + 1) * TT1],
                       pb[0][0:16, 256 * outc:256 * outc + TT1], AF.Copy,
                       rk=[PS[0]], wk=[("bax", outc)])
            tr.act(t_beta[:], t_ba[:, 0:TT1], AF.Sigmoid,
                   rk=[("bax", 0)], wk=[("beta",)])
            tr.vts(t_sp[:], t_ba[:, TT1:2 * TT1], t_dtb[:, 0:1], None, ALU.add,
                   rk=[("bax", 1), ("dtb",)], wk=[("spa",)])
            tr.vts(t_g[:], t_sp[:], 0.0, None, ALU.max,
                   rk=[("spa",)], wk=[("gmax",)])
            tr.act(t_sp[:], t_sp[:], AF.Abs, rk=[("spa",)], wk=[("spa",)])
            tr.act(t_sp[:], t_sp[:], AF.Exp, rk=[("spa",)], wk=[("spa",)],
                   scale=-1.0)
            tr.vts(t_sp[:], t_sp[:], 1.0, None, ALU.add,
                   rk=[("spa",)], wk=[("spa",)])
            tr.act(t_sp[:], t_sp[:], AF.Ln, rk=[("spa",)], wk=[("spa",)])
            tr.vtt(t_g[:], t_g[:], t_sp[:], ALU.add,
                   rk=[("gmax",), ("spa",)], wk=[("gmax",)])
            tr.vts(t_g[:], t_g[:], t_negea[:, 0:1], None, ALU.mult,
                   rk=[("gmax",), ("negea",)], wk=[("g",)])
            tr.act(t_lb[:], t_beta[:], AF.Ln, rk=[("beta",)], wk=[("lb",)])
            for cc_ in range(32):
                base = cc_ * (TT1 + 4)
                mixk, hk = ("mix", cc_), ("mixh", cc_)
                wcol = lambda j: t_convw[:, cc_ * 4 + j:cc_ * 4 + j + 1]
                tr.vts(t_cv[0][:], t_mix[:, base:base + TT1], wcol(0), None,
                       ALU.mult, rk=[mixk, hk, ("convw",)], wk=[("cv", 0)])
                tr.vts(t_cv[1][:], t_mix[:, base + 1:base + 1 + TT1], wcol(1),
                       None, ALU.mult, rk=[mixk, hk, ("convw",)],
                       wk=[("cv", 1)], eng="p")
                tr.vts(t_cv[2][:], t_mix[:, base + 2:base + 2 + TT1], wcol(2),
                       None, ALU.mult, rk=[mixk, hk, ("convw",)],
                       wk=[("cv", 2)])
                tr.vts(t_cv[3][:], t_mix[:, base + 3:base + 3 + TT1], wcol(3),
                       None, ALU.mult, rk=[mixk, ("convw",)],
                       wk=[("cv", 3)], eng="p")
                tr.vtt(t_cv[0][:], t_cv[0][:], t_cv[1][:], ALU.add,
                       rk=[("cv", 0), ("cv", 1)], wk=[("cv", 0)])
                tr.vtt(t_cv[2][:], t_cv[2][:], t_cv[3][:], ALU.add,
                       rk=[("cv", 2), ("cv", 3)], wk=[("cv", 2)], eng="p")
                tr.vtt(t_cv[0][:], t_cv[0][:], t_cv[2][:], ALU.add,
                       rk=[("cv", 0), ("cv", 2)], wk=[("cv", 0)])
                tr.act(t_qkv[:, cc_ * TT1:(cc_ + 1) * TT1], t_cv[0][:],
                       AF.Silu, rk=[("cv", 0)], wk=[("qkv", cc_)])
            for ci in range(TT1 // C):
                scan_chunk(tr, ci, ogrow, ogcol)

        def scan_chunk(tr, ci, ogrow, ogcol):
            t0 = ci * C
            tr.transpose(pb[0][:, 0:16], t_g[:, t0:t0 + C],
                         cs["ident"][0:16, 0:16],
                         rk=[("g",), ("c", "ident")], wk=PS[0])
            tr.act(t_gtok[:], pb[0][:, 0:16], AF.Copy, rk=[PS[0]],
                   wk=[("gtok",)])
            tr.transpose(pb[1][:, 0:16], t_beta[:, t0:t0 + C],
                         cs["ident"][0:16, 0:16],
                         rk=[("beta",), ("c", "ident")], wk=PS[1])
            tr.act(t_btok[:], pb[1][:, 0:16], AF.Copy, rk=[PS[1]],
                   wk=[("btok",)])
            tr.transpose(pb[2][:, 0:16], t_lb[:, t0:t0 + C],
                         cs["ident"][0:16, 0:16],
                         rk=[("lb",), ("c", "ident")], wk=PS[2])
            tr.act(t_lba[:], pb[2][:, 0:16], AF.Copy, rk=[PS[2]],
                   wk=[("lbtok",)])
            tr.mm(pb[3][:, 0:16], cs["tri_incl"][:], t_gtok[:],
                  rk=[("c", "tri_incl"), ("gtok",)], wk=PS[3])
            tr.act(t_cum[:], pb[3][:, 0:16], AF.Copy, rk=[PS[3]],
                   wk=[("cum",)])
            tr.mm(pb[4][:, 0:16], cs["tri_rev"][:], t_gtok[:],
                  rk=[("c", "tri_rev"), ("gtok",)], wk=PS[4])
            tr.act(t_dect[:], pb[4][:, 0:16], AF.Exp, rk=[PS[4]],
                   wk=[("dect",)])
            tr.vtt(t_lba[:], t_lba[:], t_cum[:], ALU.add,
                   rk=[("lbtok",), ("cum",)], wk=[("lba",)])
            tr.act(t_bacol[:], t_lba[:], AF.Exp, rk=[("lba",)], wk=[("bacol",)])
            tr.transpose(pb[5][0:16, 0:128], t_cum[:], cs["ident"][:, :],
                         rk=[("cum",), ("c", "ident")], wk=PS[5])
            tr.act(t_cumT[0:16, :], pb[5][0:16, 0:128], AF.Copy,
                   rk=[PS[5]], wk=[("cumT",)])
            tr.transpose(pb[6][0:16, 0:128], t_lba[:], cs["ident"][:, :],
                         rk=[("lba",), ("c", "ident")], wk=PS[6])
            tr.act(t_lbaT[0:16, :], pb[6][0:16, 0:128], AF.Copy,
                   rk=[PS[6]], wk=[("lbaT",)])
            for kh in range(NKH_L):
                for qk in range(2):
                    cc_ = kh if qk == 0 else 8 + kh
                    src = t_qkv[:, cc_ * TT1 + t0: cc_ * TT1 + t0 + C]
                    dst = t_qh if qk == 0 else t_kh
                    dk = ("qh", kh) if qk == 0 else ("kh", kh)
                    tr.act(t_sq[:], src, AF.Square, rk=[("qkv", cc_)],
                           wk=[("sq",)])
                    tr.mm(pb[7][0:1, qk * C:(qk + 1) * C], cs["ones_col"][:],
                          t_sq[:], rk=[("c", "ones_col"), ("sq",)], wk=PS[7])
                    tr.vts(t_rstd[0:1, qk * C:(qk + 1) * C],
                           pb[7][0:1, qk * C:(qk + 1) * C], EPS_L2, None,
                           ALU.add, rk=[PS[7]], wk=[("rstl", qk)])
                    tr.act(t_rstd[0:1, qk * C:(qk + 1) * C],
                           t_rstd[0:1, qk * C:(qk + 1) * C], AF.Ln,
                           rk=[("rstl", qk)], wk=[("rstl", qk)])
                    tr.act(t_rstd[0:1, qk * C:(qk + 1) * C],
                           t_rstd[0:1, qk * C:(qk + 1) * C], AF.Exp,
                           rk=[("rstl", qk)], wk=[("rstl", qk)], scale=-0.5)
                    tr.mm(pb[0][:, qk * C:(qk + 1) * C],
                          cs["qscale_row" if qk == 0 else "ones_row"][:],
                          t_rstd[0:1, qk * C:(qk + 1) * C],
                          rk=[("c", "ones_row"), ("rstl", qk)], wk=PS[0])
                    tr.vtt(dst[:, kh * C:(kh + 1) * C], src,
                           pb[0][:, qk * C:(qk + 1) * C], ALU.mult,
                           rk=[("qkv", cc_), PS[0]], wk=[dk])
            for rnd in range(4):
                scan_round(tr, rnd, ci, ogrow, ogcol)
            if ci == TT1 // C - 1:
                rtmp_ = tr.it_regs["tmp"]
                for half in range(2):
                    tr.raw(nc.sync.reg_add, rtmp_, tr.it_regs["rx"],
                           half * 8 * 128 * T)
                    dst = bass.AP(d_oga, rtmp_,
                                  [[T, 128], [128 * T, 8], [1, TT1]])
                    tr.store(dst,
                             t_ogt[:, half * 8 * TT1:(half + 1) * 8 * TT1],
                             rk=[("ogt", v, cc)
                                 for v in range(half * 8, half * 8 + 8)
                                 for cc in (0, 1)], ch="ogst")

        def scan_round(tr, rnd, ci, ogrow, ogcol):
            vhs = [rnd * 4 + j for j in range(4)]
            khs = [rnd * 2, rnd * 2 + 1]
            t0 = ci * C
            for j, kh in enumerate(khs):
                ksl = t_kh[:, kh * C:(kh + 1) * C]
                qsl = t_qh[:, kh * C:(kh + 1) * C]
                tr.mm(pb[0][:, j * C:(j + 1) * C], ksl, ksl,
                      rk=[("kh", kh)], wk=PS[0])
                tr.mm(pb[0][:, 256 + j * C:256 + (j + 1) * C], ksl, qsl,
                      rk=[("kh", kh), ("qh", kh)], wk=PS[0])
                tr.act(t_G0[j][:], pb[0][:, j * C:(j + 1) * C], AF.Copy,
                       rk=[PS[0]], wk=[("G0", j)], scale=-1.0)
                tr.act(t_G1[j][:], pb[0][:, 256 + j * C:256 + (j + 1) * C],
                       AF.Copy, rk=[PS[0]], wk=[("G1", j)])
                tr.transpose(pb[1][:, j * C:(j + 1) * C], ksl, cs["ident"][:],
                             rk=[("kh", kh), ("c", "ident")], wk=PS[1])
                tr.act(t_Kt[j][:], pb[1][:, j * C:(j + 1) * C], AF.Copy,
                       rk=[PS[1]], wk=[("Kt", j)])
            for j, v in enumerate(vhs):
                eh = cs["eh"][:, v * C:(v + 1) * C]
                bnk = 2 + j
                tr.mm(pb[bnk][:, 0:C], eh, t_lbaT[0:16, :],
                      rk=[("c", "eh"), ("lbaT",)], wk=PS[bnk])
                tr.mm(pb[bnk][:, C:2 * C], eh, t_cumT[0:16, :],
                      rk=[("c", "eh"), ("cumT",)], wk=PS[bnk])
                tr.vts(t_DB[j][:], pb[bnk][:, 0:C], t_cum[:, v:v + 1], None,
                       ALU.subtract, rk=[PS[bnk], ("cum",)], wk=[("DB", j)])
                tr.vtt(t_DB[j][:], t_DB[j][:], cs["maskn_s"][:], ALU.add,
                       rk=[("DB", j), ("c", "maskn_s")], wk=[("DB", j)])
                tr.act(t_DB[j][:], t_DB[j][:], AF.Exp, rk=[("DB", j)],
                       wk=[("DB", j)])
                tr.vts(t_Dq[j][:], pb[bnk][:, C:2 * C], t_cum[:, v:v + 1],
                       None, ALU.subtract, rk=[PS[bnk], ("cum",)],
                       wk=[("Dq", j)])
                tr.vtt(t_Dq[j][:], t_Dq[j][:], cs["maskn_i"][:], ALU.add,
                       rk=[("Dq", j), ("c", "maskn_i")], wk=[("Dq", j)])
                tr.act(t_Dq[j][:], t_Dq[j][:], AF.Exp, rk=[("Dq", j)],
                       wk=[("Dq", j)])
                tr.vts(t_DBz[j][:], pb[bnk][:, C:2 * C], t_lba[:, v:v + 1],
                       None, ALU.subtract, rk=[PS[bnk], ("lba",)],
                       wk=[("DBz", j)])
                tr.vts(t_DBz[j][:], t_DBz[j][:], -1.0, None, ALU.mult,
                       rk=[("DBz", j)], wk=[("DBz", j)])
                tr.vtt(t_DBz[j][:], t_DBz[j][:], cs["maskn_sT"][:], ALU.add,
                       rk=[("DBz", j), ("c", "maskn_sT")], wk=[("DBz", j)])
                tr.act(t_DBz[j][:], t_DBz[j][:], AF.Exp, rk=[("DBz", j)],
                       wk=[("DBz", j)])
                tr.act(t_bA[j][:], pb[bnk][:, C:2 * C], AF.Exp,
                       rk=[PS[bnk]], wk=[("bA", j)])
                kj = j // 2
                tr.vtt(t_Y[j][0][:], t_DB[j][:], t_G0[kj][:], ALU.mult,
                       rk=[("DB", j), ("G0", kj)], wk=[("Y", j, 0)], eng="p")
                tr.vtt(t_Z[j][0][:], t_DBz[j][:], t_G0[kj][:], ALU.mult,
                       rk=[("DBz", j), ("G0", kj)], wk=[("Z", j, 0)], eng="p")
                tr.vtt(t_P[j][:], t_Dq[j][:], t_G1[kj][:], ALU.mult,
                       rk=[("Dq", j), ("G1", kj)], wk=[("P", j)], eng="p")
                tr.vtt(t_Qa[j][:], t_qh[:, (v // 2) * C:(v // 2 + 1) * C],
                       t_bA[j][:], ALU.mult,
                       rk=[("qh", v // 2), ("bA", j)], wk=[("Qa", j)], eng="p")
            for j, kh in enumerate(khs):
                tr.mm(pb[6][:, j * 256:(j + 1) * 256],
                      t_kh[:, kh * C:(kh + 1) * C],
                      t_state[:, kh * 256:(kh + 1) * 256],
                      rk=[("kh", kh), ("sv", 2 * kh), ("sv", 2 * kh + 1)],
                      wk=PS[6])
            for j, v in enumerate(vhs):
                vc = 16 + v
                tr.transpose(pb[2 + j][:, 2 * C:3 * C],
                             t_qkv[:, vc * TT1 + t0: vc * TT1 + t0 + C],
                             cs["ident"][:],
                             rk=[("qkv", vc), ("c", "ident")], wk=PS[2 + j])
                tr.vts(t_r1[j][:], pb[2 + j][:, 2 * C:3 * C],
                       t_btok[:, v:v + 1], None, ALU.mult,
                       rk=[PS[2 + j], ("btok",)], wk=[("r1", j)])
                kj = j // 2
                tr.vts(t_r2[j][:],
                       pb[6][:, kj * 256 + (v % 2) * C:
                             kj * 256 + (v % 2) * C + C],
                       t_bacol[:, v:v + 1], None, ALU.mult,
                       rk=[PS[6], ("bacol",)], wk=[("r2", j)])
                tr.vtt(t_U[j][0][:], t_r1[j][:], t_r2[j][:], ALU.subtract,
                       rk=[("r1", j), ("r2", j)], wk=[("U", j, 0)])
            for lev in range(7):
                yb = lev % 2
                ub = lev % 2
                for j in range(4):
                    tr.mm(pb[j][:, 0:C], t_Y[j][yb][:], t_U[j][ub][:],
                          rk=[("Y", j, yb), ("U", j, ub)], wk=PS[j])
                for j in range(4):
                    tr.vtt(t_U[j][1 - ub][:], pb[j][:, 0:C],
                           t_U[j][ub][:], ALU.add,
                           rk=[PS[j], ("U", j, ub)], wk=[("U", j, 1 - ub)])
                if lev < 6:
                    for j in range(4):
                        tr.mm(pb[4 + j][:, 0:C],
                              t_Z[j][yb][:], t_Y[j][yb][:],
                              rk=[("Z", j, yb), ("Y", j, yb)], wk=PS[4 + j])
                    for j in range(4):
                        tr.act(t_Y[j][1 - yb][:], pb[4 + j][:, 0:C],
                               AF.Copy, rk=[PS[4 + j]], wk=[("Y", j, 1 - yb)])
                    if lev < 5:
                        for j in range(4):
                            tr.mm(pb[j][:, C:2 * C],
                                  t_Y[j][yb][:], t_Z[j][yb][:],
                                  rk=[("Y", j, yb), ("Z", j, yb)], wk=PS[j])
                        for j in range(4):
                            tr.act(t_Z[j][1 - yb][:],
                                   pb[j][:, C:2 * C], AF.Copy,
                                   rk=[PS[j]], wk=[("Z", j, 1 - yb)])
            UF = 1
            for j, v in enumerate(vhs):
                tr.transpose(pb[4 + j][:, 0:C],
                             t_z[:, v * TT1 + t0:v * TT1 + t0 + C],
                             cs["ident"][:],
                             rk=[("z", v), ("c", "ident")], wk=PS[4 + j])
            for j, v in enumerate(vhs):
                tr.act(t_sz[j][:], pb[4 + j][:, 0:C], AF.Silu,
                       rk=[PS[4 + j]], wk=[("szj", j)])
            for j, v in enumerate(vhs):
                tr.mm(pb[j][:, 0:C], t_Qa[j][:],
                      t_state[:, v * C:(v + 1) * C],
                      rk=[("Qa", j), ("sv", v)], wk=PS[j],
                      start=True, stop=False)
                tr.mm(pb[j][:, 0:C], t_P[j][:], t_U[j][UF][:],
                      rk=[("P", j), ("U", j, UF)], wk=PS[j],
                      start=False, stop=True)
            for j, v in enumerate(vhs):
                tr.vts(t_r1[j][:], t_Kt[j // 2][:], t_dect[:, v:v + 1], None,
                       ALU.mult, rk=[("Kt", j // 2), ("dect",)],
                       wk=[("r1", j)])
                tr.mm(pb[4 + j][:, 0:C], t_r1[j][:], t_U[j][UF][:],
                      rk=[("r1", j), ("U", j, UF)], wk=PS[4 + j])
                tr.vts(t_r2[j][:], t_state[:, v * C:(v + 1) * C],
                       t_bA[j][:, C - 1:C], None, ALU.mult,
                       rk=[("sv", v), ("bA", j)], wk=[("r2", j)])
                tr.vtt(t_state[:, v * C:(v + 1) * C],
                       pb[4 + j][:, 0:C], t_r2[j][:], ALU.add,
                       rk=[PS[4 + j], ("r2", j)], wk=[("sv", v)])
            for j, v in enumerate(vhs):
                tr.vtt(t_og0[j][:], pb[j][:, 0:C], t_sz[j][:],
                       ALU.mult, rk=[PS[j], ("szj", j)], wk=[("og0", j)])
                tr.act(t_msq0[:], t_og0[j][:], AF.Square,
                       rk=[("og0", j)], wk=[("msq",), ("ms", j)],
                       accum_out=t_ms[j][:])
                tr.vts(t_rst[j][:], t_ms[j][:], 1.0 / 128.0, EPS_RMS,
                       ALU.mult, rk=[("ms", j)], wk=[("rst", j)],
                       op1=ALU.add)
                tr.act(t_rst[j][:], t_rst[j][:], AF.Ln, rk=[("rst", j)],
                       wk=[("rst", j)])
                tr.act(t_rst[j][:], t_rst[j][:], AF.Exp, rk=[("rst", j)],
                       wk=[("rst", j)], scale=-0.5)
                tr.vts(t_og0[j][:], t_og0[j][:], t_rst[j][:, 0:1], None,
                       ALU.mult, rk=[("og0", j), ("rst", j)], wk=[("og0", j)])
                tr.vtt(t_og0[j][:], t_og0[j][:], t_nwb[:], ALU.mult,
                       rk=[("og0", j), ("nwb",)], wk=[("og0", j)], eng="p")
                tr.transpose(pb[j][:, C:2 * C], t_og0[j][:],
                             cs["ident"][:],
                             rk=[("og0", j), ("c", "ident")], wk=PS[j])
                tr.act(t_ogt[:, v * TT1 + t0:v * TT1 + t0 + C],
                       pb[j][:, C:2 * C], AF.Copy,
                       rk=[PS[j]], wk=[("ogt", v, ci)])


        def ln_fm(tr, src, dst, sk, dk, gamma=None, beta=None, store_c2=None):
            for mc in range(16):
                tr.mm(pb[0][0:1, 0:TT2], cs["ones_col"][:],
                      src[:, mc * TT2:(mc + 1) * TT2],
                      rk=[("c", "ones_col"), (sk, mc)], wk=PS[0],
                      start=(mc == 0), stop=(mc == 15))
            for mc in range(16):
                sq = t_sqa[:, (mc % 2) * TT2:(mc % 2) * TT2 + TT2]
                tr.act(sq, src[:, mc * TT2:(mc + 1) * TT2], AF.Square,
                       rk=[(sk, mc)], wk=[("sqa", mc % 2)])
                tr.mm(pb[1][0:1, 0:TT2], cs["ones_col"][:], sq,
                      rk=[("c", "ones_col"), ("sqa", mc % 2)], wk=PS[1],
                      start=(mc == 0), stop=(mc == 15))
            tr.vts(t_st1[:], pb[0][0:1, 0:TT2], 1.0 / HID, None, ALU.mult,
                   rk=[PS[0]], wk=[("mu",)])
            tr.vts(t_st2[:], pb[1][0:1, 0:TT2], 1.0 / HID, None, ALU.mult,
                   rk=[PS[1]], wk=[("var",)])
            tr.vtt(t_st3[:], t_st1[:], t_st1[:], ALU.mult,
                   rk=[("mu",)], wk=[("mu2",)])
            tr.vtt(t_st2[:], t_st2[:], t_st3[:], ALU.subtract,
                   rk=[("var",), ("mu2",)], wk=[("var",)])
            tr.vts(t_st2[:], t_st2[:], EPS_LN, None, ALU.add,
                   rk=[("var",)], wk=[("var",)])
            tr.act(t_st2[:], t_st2[:], AF.Ln, rk=[("var",)], wk=[("var",)])
            tr.act(t_st2[:], t_st2[:], AF.Exp, rk=[("var",)], wk=[("var",)],
                   scale=-0.5)
            tr.mm(pb[2][:, 0:TT2], cs["ones_row"][:], t_st1[:],
                  rk=[("c", "ones_row"), ("mu",)], wk=PS[2])
            tr.mm(pb[3][:, 0:TT2], cs["ones_row"][:], t_st2[:],
                  rk=[("c", "ones_row"), ("var",)], wk=PS[3])
            for mc in range(16):
                tmp = t_sqa[:, (mc % 2) * TT2:(mc % 2) * TT2 + TT2]
                tr.vtt(tmp, src[:, mc * TT2:(mc + 1) * TT2], pb[2][:, 0:TT2],
                       ALU.subtract, rk=[(sk, mc), PS[2]],
                       wk=[("sqa", mc % 2)])
                if store_c2 is not None:
                    ob = t_o2[:, (mc % 4) * TT2:(mc % 4) * TT2 + TT2]
                    tr.vtt(ob, tmp, pb[3][:, 0:TT2], ALU.mult,
                           rk=[("sqa", mc % 2), PS[3]], wk=[("o2", mc % 4)])
                    if mc % 4 == 3:
                        rg = tr.it_regs2
                        tr.raw(nc.sync.reg_add, rg["tmp2"], rg["rc2"],
                               (mc - 3) * 128 * TOW)
                        dap = bass.AP(d_out, rg["tmp2"],
                                      [[TOW, 128], [128 * TOW, 4], [1, TT2]])
                        tr.store(dap, t_o2[:, 0:4 * TT2],
                                 rk=[("o2", i) for i in range(4)], ch="outst")
                elif gamma is not None:
                    tr.vtt(tmp, tmp, pb[3][:, 0:TT2], ALU.mult,
                           rk=[("sqa", mc % 2), PS[3]], wk=[("sqa", mc % 2)])
                    tr.vts(dst[:, mc * TT2:(mc + 1) * TT2], tmp,
                           gamma[:, mc:mc + 1], beta[:, mc:mc + 1], ALU.mult,
                           rk=[("sqa", mc % 2), ("gam",), ("bet",)],
                           wk=[(dk, mc)], op1=ALU.add)
                else:
                    tr.vtt(dst[:, mc * TT2:(mc + 1) * TT2], tmp,
                           pb[3][:, 0:TT2], ALU.mult,
                           rk=[("sqa", mc % 2), PS[3]], wk=[(dk, mc)])

        def p2_body(tr, it):
            rg = it["regs"]
            tr.it_regs2 = rg
            rc2_, rtog_, rt2_ = rg["rc2"], rg["rtog"], rg["tmp2"]
            for mh in range(2):
                tr.raw(nc.sync.reg_add, rt2_, rc2_, mh * 8 * 128 * TOW)
                xap = bass.AP(d_xown, rt2_,
                              [[TOW, 128], [128 * TOW, 8], [1, TT2]])
                tr.load(t_xr[:, 0:8 * TT2], xap,
                        wk=[("xr", i) for i in range(8)], ch="xr")
                for s in range(4):
                    sb_ = s % 2
                    tr.raw(nc.sync.reg_add, rt2_, rtog_,
                           s * 8 * 128 * T)
                    oap = bass.AP(d_ogf, rt2_,
                                  [[T, 128], [128 * T, 8], [1, TT2]])
                    strip = t_og2[:, sb_ * 8 * TT2:(sb_ + 1) * 8 * TT2]
                    tr.load(strip, oap, wk=[("og2", sb_)], ch=f"og2{sb_}")
                    for kk in range(8):
                        k = s * 8 + kk
                        wb_ = k % 4
                        slab = t_w2[:, wb_ * 1024:wb_ * 1024 + 1024]
                        tr.load(slab, d_wout[k * 128:(k + 1) * 128,
                                             mh * 1024:(mh + 1) * 1024],
                                wk=[("w2", wb_)], ch=f"w2{wb_}")
                        for m in range(8):
                            tr.mm(pb[m][:, 0:TT2],
                                  slab[:, m * 128:(m + 1) * 128],
                                  strip[:, kk * TT2:(kk + 1) * TT2],
                                  rk=[("w2", wb_), ("og2", sb_)],
                                  wk=PS[m], start=(k == 0), stop=(k == 31))
                for m in range(8):
                    mc = mh * 8 + m
                    tr.vtt(t_y2[:, mc * TT2:(mc + 1) * TT2], pb[m][:, 0:TT2],
                           t_xr[:, m * TT2:(m + 1) * TT2], ALU.add,
                           rk=[PS[m], ("xr", m)], wk=[("y2", mc)])
            ln_fm(tr, t_y2, t_x1, "y2", "x1")
            for kc in range(16):
                wb_ = kc % 4
                slab = t_w2[:, wb_ * 1024:wb_ * 1024 + 8]
                tr.load(slab, d_gate[kc * 128:(kc + 1) * 128, :],
                        wk=[("w2", wb_)], ch=f"w2{wb_}")
                tr.mm(pb[4][0:8, 0:TT2], slab,
                      t_x1[:, kc * TT2:(kc + 1) * TT2],
                      rk=[("w2", wb_), ("x1", kc)], wk=PS[4],
                      start=(kc == 0), stop=(kc == 15))
            tr.act(t_lg[:], pb[4][0:8, 0:TT2], AF.Copy, rk=[PS[4]],
                   wk=[("lg",)])
            for tc in range(4):
                tr.transpose(pb[5][:, tc * 8:(tc + 1) * 8],
                             t_lg[:, tc * 128:(tc + 1) * 128],
                             cs["ident"][0:8, 0:8],
                             rk=[("lg",), ("c", "ident")], wk=PS[5])
                tr.act(t_ltok[tc][:], pb[5][:, tc * 8:(tc + 1) * 8], AF.Copy,
                       rk=[PS[5]], wk=[("ltok", tc)])
                tr.vts(t_msk[tc][:], t_ltok[tc][:], 0.0, None, ALU.add,
                       rk=[("ltok", tc)], wk=[("msk", tc), ("m1", tc)],
                       op1=ALU.max, accum_out=t_m1[tc][:])
                tr.vts(t_msk[tc][:], t_ltok[tc][:], t_m1[tc][:, 0:1], None,
                       ALU.is_ge, rk=[("ltok", tc), ("m1", tc)],
                       wk=[("msk", tc)])
                tr.vts(t_msk[tc][:], t_msk[tc][:], 0.0, None, ALU.add,
                       rk=[("msk", tc)], wk=[("msk", tc), ("cnt", tc)],
                       op1=ALU.add, accum_out=t_cnt[tc][:])
                tr.vts(t_msk[tc][:], t_msk[tc][:], 1e30, None, ALU.mult,
                       rk=[("msk", tc)], wk=[("msk", tc)])
                tr.vtt(t_msk[tc][:], t_ltok[tc][:], t_msk[tc][:],
                       ALU.subtract, rk=[("ltok", tc), ("msk", tc)],
                       wk=[("msk", tc)])
                tr.vts(t_msk[tc][:], t_msk[tc][:], 0.0, None, ALU.add,
                       rk=[("msk", tc)], wk=[("msk", tc), ("m2c", tc)],
                       op1=ALU.max, accum_out=t_m2c[tc][:])
                tr.vts(t_cnt[tc][:], t_cnt[tc][:], 2.0, None, ALU.is_ge,
                       rk=[("cnt", tc)], wk=[("cnt", tc)])
                tr.vtt(t_d1[tc][:], t_m1[tc][:], t_m2c[tc][:], ALU.subtract,
                       rk=[("m1", tc), ("m2c", tc)], wk=[("d1", tc)])
                tr.vtt(t_d1[tc][:], t_d1[tc][:], t_cnt[tc][:], ALU.mult,
                       rk=[("d1", tc), ("cnt", tc)], wk=[("d1", tc)])
                tr.vtt(t_m1[tc][:], t_m1[tc][:], t_m2c[tc][:], ALU.add,
                       rk=[("m1", tc), ("m2c", tc)], wk=[("m1", tc)])
                tr.vtt(t_m1[tc][:], t_m1[tc][:], t_d1[tc][:], ALU.add,
                       rk=[("m1", tc), ("d1", tc)], wk=[("m1", tc)])
                tr.transpose(pb[6][0:1, tc * 128:(tc + 1) * 128],
                             t_m1[tc][:], cs["ident"][:],
                             rk=[("m1", tc), ("c", "ident")], wk=PS[6])
                tr.act(t_t2r[0:1, tc * 128:(tc + 1) * 128],
                       pb[6][0:1, tc * 128:(tc + 1) * 128], AF.Copy,
                       rk=[PS[6]], wk=[("t2r", tc)])
            tr.mm(pb[7][:, 0:TT2], cs["ones_row"][:], t_t2r[0:1, :],
                  rk=[("c", "ones_row"), ("t2r", 0), ("t2r", 1), ("t2r", 2),
                      ("t2r", 3)], wk=PS[7])
            tr.vcopy(t_t2b[:], pb[7][:, 0:TT2], rk=[PS[7]], wk=[("t2b",)])
            for mh in range(2):
                for kc in range(16):
                    wb_ = kc % 4
                    slab = t_w2[:, wb_ * 1024:wb_ * 1024 + 1024]
                    tr.load(slab, d_expert[kc * 128:(kc + 1) * 128,
                                           mh * 1024:(mh + 1) * 1024],
                            wk=[("w2", wb_)], ch=f"w2{wb_}")
                    for m in range(8):
                        tr.mm(pb[m][:, 0:TT2], slab[:, m * 128:(m + 1) * 128],
                              t_x1[:, kc * TT2:(kc + 1) * TT2],
                              rk=[("w2", wb_), ("x1", kc)], wk=PS[m],
                              start=(kc == 0), stop=(kc == 15))
                for m in range(8):
                    mc = mh * 8 + m
                    tr.vtt(t_y2[:, mc * TT2:(mc + 1) * TT2], pb[m][:, 0:TT2],
                           t_t2b[:], ALU.mult, rk=[PS[m], ("t2b",)],
                           wk=[("y2", mc)])
            for o in range(8):
                for mc in range(8):
                    bm = mc % 2
                    for kc in range(16):
                        wb_ = kc % 4
                        slab = t_w2[:, wb_ * 1024:wb_ * 1024 + 128]
                        tr.load(slab, d_up[kc * 128:(kc + 1) * 128,
                                           (o * 8 + mc) * 128:
                                           (o * 8 + mc + 1) * 128],
                                wk=[("w2", wb_)], ch=f"w2{wb_}")
                        tr.mm(pb[bm][:, 0:TT2], slab,
                              t_x1[:, kc * TT2:(kc + 1) * TT2],
                              rk=[("w2", wb_), ("x1", kc)], wk=PS[bm],
                              start=(kc == 0), stop=(kc == 15))
                    tr.act(t_sqa[:, 0:TT2], pb[bm][:, 0:TT2], AF.Relu,
                           rk=[PS[bm]], wk=[("sqa", 0)])
                    tr.vtt(t_h2[:, mc * TT2:(mc + 1) * TT2], t_sqa[:, 0:TT2],
                           t_sqa[:, 0:TT2], ALU.mult,
                           rk=[("sqa", 0)], wk=[("h2", mc)])
                for mh in range(2):
                    for kc in range(8):
                        wb_ = kc % 4
                        slab = t_w2[:, wb_ * 1024:wb_ * 1024 + 1024]
                        tr.load(slab, d_down[(o * 8 + kc) * 128:
                                             (o * 8 + kc + 1) * 128,
                                             mh * 1024:(mh + 1) * 1024],
                                wk=[("w2", wb_)], ch=f"w2{wb_}")
                        for m in range(8):
                            tr.mm(pb[m][:, 0:TT2],
                                  slab[:, m * 128:(m + 1) * 128],
                                  t_h2[:, kc * TT2:(kc + 1) * TT2],
                                  rk=[("w2", wb_), ("h2", kc)], wk=PS[m],
                                  start=(kc == 0), stop=(kc == 7))
                    for m in range(8):
                        mc = mh * 8 + m
                        tr.vtt(t_y2[:, mc * TT2:(mc + 1) * TT2],
                               pb[m][:, 0:TT2],
                               t_y2[:, mc * TT2:(mc + 1) * TT2], ALU.add,
                               rk=[PS[m], ("y2", mc)], wk=[("y2", mc)])
            ln_fm(tr, t_y2, t_y2, "y2", "y2", gamma=t_gam, beta=t_bet)
            for mc in range(16):
                tr.vtt(t_y2[:, mc * TT2:(mc + 1) * TT2],
                       t_y2[:, mc * TT2:(mc + 1) * TT2],
                       t_x1[:, mc * TT2:(mc + 1) * TT2], ALU.add,
                       rk=[("y2", mc), ("x1", mc)], wk=[("y2", mc)])
            ln_fm(tr, t_y2, None, "y2", None, store_c2=True)

        def emit_body(body_fn, it, needed=None):
            record = needed is None
            if record:
                needed = {e: set() for e in
                          ("t", "v", "a", "s", "p", "ld", "st")}
            tr = Tracker(nc, sems, needed, record)
            body_fn(tr, it)
            tr.wait_all_dma()
            return needed

        def clear_sems():
            nc.all_engine_barrier()
            for e in ("t", "v", "a", "s", "p"):
                nc.gpsimd.sem_clear(sems[e])
            for ch in DMACH:
                nc.gpsimd.sem_clear(sems["dma:" + ch])
            nc.all_engine_barrier()

        needed = emit_body(lambda tr, it: prologue(tr), {})
        emit_body(lambda tr, it: prologue(tr), {}, needed)
        clear_sems()

        rx = nc.sync.alloc_register("r_xoff")
        rtmp = nc.sync.alloc_register("r_tmp")
        rbase = nc.sync.alloc_register("r_base")
        rc2 = nc.sync.alloc_register("r_c2")
        rtog = nc.sync.alloc_register("r_tog")
        nc.sync.reg_mov(rx, 0)
        nc.all_engine_barrier()
        with nc.Fori(0, nt1, engines=mybir.ALL_ENGINES):
            it = {"ogrow": 0, "ogcol": 0,
                  "regs": {"tmp": rtmp, "rx": rx}}
            needed = emit_body(p1_body, it)
            emit_body(p1_body, it, needed)
            nc.sync.reg_add(rx, rx, TT1)
            clear_sems()

        nc.gpsimd.collective_compute(
            "AllGather", ALU.bypass,
            replica_groups=[[0, 1], [2, 3], [4, 5], [6, 7]],
            ins=[d_oga.ap().opt()], outs=[d_ogf.ap().opt()],
        ).then_inc(sems["cc"])
        nc.gpsimd.wait_ge(sems["cc"], 1)
        if taps:
            nc.gpsimd.dma_start(out=d_tap[:, :], in_=d_ogf[:, :]).then_inc(
                sems["cc"], 16)
            nc.gpsimd.wait_ge(sems["cc"], 17)
        nc.all_engine_barrier()

        nc.sync.reg_load(rbase, t_tb[0:1, 0:1])
        nc.sync.reg_mov(rc2, 0)
        nc.all_engine_barrier()
        rtmp2 = nc.sync.alloc_register("r_tmp2")
        with nc.Fori(0, nt2, engines=mybir.ALL_ENGINES):
            nc.sync.reg_add(rtog, rbase, rc2)
            it = {"regs": {"rc2": rc2, "rtog": rtog, "tmp2": rtmp2}}
            needed = emit_body(p2_body, it)
            emit_body(p2_body, it, needed)
            nc.sync.reg_add(rc2, rc2, TT2)
            clear_sems()
        nc.all_engine_barrier()
    return nc


def make_inmaps(inp):
    x = np.asarray(inp["L0_attn_hidden_states"])
    wqkvz = np.asarray(inp["L0_attn_W_qkvz"])
    wba = np.asarray(inp["L0_attn_W_ba"])
    convw = np.asarray(inp["L0_attn_conv_w"])
    dtb = np.asarray(inp["L0_attn_dt_bias"])
    alog = np.asarray(inp["L0_attn_A_log"])
    nw = np.asarray(inp["L0_attn_norm_weight"])
    wout = np.asarray(inp["L0_attn_W_out"])
    gate = np.asarray(inp["L0_ffn_gate_w"])
    up = np.asarray(inp["L0_ffn_up_w"])
    down = np.asarray(inp["L0_ffn_down_w"])
    expert = np.asarray(inp["L0_ffn_expert_weight"])
    gam = np.asarray(inp["L0_ffn_ln_gamma"])
    bet = np.asarray(inp["L0_ffn_ln_beta"])
    consts = host_constants()
    in_maps = []
    for c in range(8):
        b, g = c // 2, c % 2
        cols = slice(768 * 8 * g, 768 * 8 * (g + 1))
        bcols = [32 * g + 4 * j + r for j in range(8) for r in (0, 1)]
        acols = [32 * g + 4 * j + 2 + r for j in range(8) for r in (0, 1)]
        qrows = np.arange(1024 * g, 1024 * g + 1024)
        krows = np.arange(2048 + 1024 * g, 2048 + 1024 * g + 1024)
        vrows = np.arange(4096 + 2048 * g, 4096 + 2048 * g + 2048)
        convl = convw[np.concatenate([qrows, krows, vrows])]
        convl = np.ascontiguousarray(
            convl.reshape(32, 128, 4).transpose(1, 0, 2).reshape(128, 128))
        m = {
            "x_fm": np.ascontiguousarray(x[b].T),
            "x_own_fm": np.ascontiguousarray(x[b, 4096 * g:4096 * (g + 1)].T),
            "wqkvz": np.ascontiguousarray(wqkvz[:, cols]),
            "wb": np.ascontiguousarray(wba[:, bcols]),
            "wa": np.ascontiguousarray(wba[:, acols]),
            "convw": convl,
            "dtb": np.ascontiguousarray(dtb[16 * g:16 * (g + 1)]).reshape(16, 1),
            "alog": np.ascontiguousarray(alog[16 * g:16 * (g + 1)]).reshape(16, 1),
            "nw": np.ascontiguousarray(nw).reshape(1, 128),
            "wout": np.ascontiguousarray(wout),
            "gate": np.ascontiguousarray(gate),
            "up": np.ascontiguousarray(up),
            "down": np.ascontiguousarray(down),
            "expert": np.ascontiguousarray(expert),
            "gam": np.ascontiguousarray(gam.reshape(16, 128).T),
            "bet": np.ascontiguousarray(bet.reshape(16, 128).T),
        }
        m["tokbase"] = np.array([[4096 * g]], np.int32)
        for k, v in consts.items():
            m["c_" + k] = v
        in_maps.append(m)
    return in_maps


def kernel(**inputs):
    in_maps = make_inmaps(inputs)
    with allow_ici_all_to_all():
        nc = build_program()
        res = run_bass_kernel_spmd(nc, in_maps, core_ids=list(range(8)))
    out = np.empty((B, T, HID), np.float32)
    for c in range(8):
        b, g = c // 2, c % 2
        out[b, 4096 * g:4096 * (g + 1), :] = res.results[c]["out_fm"].T
    return out


# version 06
# speedup vs baseline: 1.1070x; 1.1070x over previous
import contextlib
import numpy as np

import concourse.bass as bass
import concourse.mybir as mybir
from concourse.bass_utils import run_bass_kernel_spmd
from concourse.aot_env import allow_ici_all_to_all

F32 = mybir.dt.float32
AF = mybir.ActivationFunctionType
ALU = mybir.AluOpType

B, T, HID = 4, 8192, 2048
NKH_L, NVH_L = 8, 16
DK = DV = 128
C = 128
TT1 = 256
NT1 = T // TT1
TT2 = 512
TOW = T // 2
NT2 = TOW // TT2
QKVZ_L = NKH_L * 768
INTER = 8192
EPS_L2 = 1e-6
EPS_RMS = 1e-6
EPS_LN = 1e-5
NBIG = -1e30


class Tracker:

    def __init__(self, nc, sems, needed, record):
        self.nc = nc
        self.sems = sems
        self.needed = needed
        self.record = record
        self.cnt = {e: 0 for e in ("t", "v", "a", "s", "p")}
        self.dcnt = {}
        self.waited = {}
        self.last_w = {}
        self.readers = {}
        self.incmap = {}
        if not record:
            for e, s in self.needed.items():
                self.incmap[e] = {c: i + 1 for i, c in enumerate(sorted(s))}

    def eng_obj(self, e):
        nc = self.nc
        return {"t": nc.tensor, "v": nc.vector, "a": nc.scalar,
                "s": nc.sync, "p": nc.gpsimd}[e]

    def _wait(self, consumer, prod_eng, prod_cnt):
        if prod_cnt <= 0:
            return
        if prod_eng == consumer and prod_eng in ("t", "s"):
            return
        key = (consumer, prod_eng)
        if self.waited.get(key, 0) >= prod_cnt:
            return
        self.waited[key] = prod_cnt
        if prod_eng.startswith("dma:"):
            if not self.record:
                self.eng_obj(consumer).wait_ge(self.sems[prod_eng],
                                               prod_cnt * 16)
            return
        if self.record:
            self.needed[prod_eng].add(prod_cnt)
            return
        val = self.incmap[prod_eng][prod_cnt]
        self.eng_obj(consumer).wait_ge(self.sems[prod_eng], val)

    def dep(self, consumer, reads=(), writes=()):
        for k in reads:
            lw = self.last_w.get(k)
            if lw:
                self._wait(consumer, lw[0], lw[1])
        for k in writes:
            lw = self.last_w.get(k)
            if lw:
                self._wait(consumer, lw[0], lw[1])
            for (re, rc) in self.readers.get(k, ()):
                self._wait(consumer, re, rc)

    def done(self, ce, instr, reads=(), writes=()):
        if ce.startswith("dma:"):
            self.dcnt[ce] = c = self.dcnt.get(ce, 0) + 1
            if not self.record and instr is not None:
                instr.then_inc(self.sems[ce], 16)
        else:
            self.cnt[ce] += 1
            c = self.cnt[ce]
            if not self.record and instr is not None and c in self.needed[ce]:
                instr.then_inc(self.sems[ce], 1)
        for k in reads:
            self.readers.setdefault(k, []).append((ce, c))
        for k in writes:
            self.last_w[k] = (ce, c)
            self.readers[k] = []
        return c

    def mm(self, out_ap, lhsT, rhs, rk, wk, start=True, stop=True):
        rk = tuple(rk)
        self.dep("t", reads=rk, writes=(wk,) if start else ())
        i = None
        if not self.record:
            i = self.nc.tensor.matmul(out_ap, lhsT, rhs, start=start, stop=stop)
        self.done("t", i, reads=rk, writes=(wk,) if stop else ())

    def transpose(self, out_ap, in_ap, ident, rk, wk):
        rk = tuple(rk)
        self.dep("t", reads=rk, writes=(wk,))
        i = None if self.record else self.nc.tensor.transpose(out_ap, in_ap, ident)
        self.done("t", i, reads=rk, writes=(wk,))

    def act(self, out_ap, in_ap, func, rk, wk, accum_out=None, **kw):
        rk, wk = tuple(rk), tuple(wk)
        self.dep("a", reads=rk, writes=wk)
        i = None
        if not self.record:
            i = self.nc.scalar.activation(out_ap, in_ap, func,
                                          accum_out=accum_out, **kw)
        self.done("a", i, reads=rk, writes=wk)

    def vtt(self, out_ap, a_ap, b_ap, op, rk, wk, eng="v"):
        rk, wk = tuple(rk), tuple(wk)
        self.dep(eng, reads=rk, writes=wk)
        i = None if self.record else self.eng_obj(eng).tensor_tensor(
            out_ap, a_ap, b_ap, op)
        self.done(eng, i, reads=rk, writes=wk)

    def vts(self, out_ap, in_ap, s1, s2, op0, rk, wk, op1=None, eng="v",
            accum_out=None):
        rk, wk = tuple(rk), tuple(wk)
        self.dep(eng, reads=rk, writes=wk)
        i = None
        if not self.record:
            if op1 is None:
                i = self.eng_obj(eng).tensor_scalar(out_ap, in_ap, s1, s2, op0)
            else:
                i = self.eng_obj(eng).tensor_scalar(out_ap, in_ap, s1, s2,
                                                    op0, op1,
                                                    accum_out=accum_out)
        self.done(eng, i, reads=rk, writes=wk)

    def vcopy(self, out_ap, in_ap, rk, wk, eng="v"):
        rk, wk = tuple(rk), tuple(wk)
        self.dep(eng, reads=rk, writes=wk)
        i = None if self.record else self.eng_obj(eng).tensor_copy(out_ap, in_ap)
        self.done(eng, i, reads=rk, writes=wk)

    def vmax(self, out_ap, in_ap, rk, wk):
        rk, wk = tuple(rk), tuple(wk)
        self.dep("v", reads=rk, writes=wk)
        i = None if self.record else self.nc.vector.max(out_ap, in_ap)
        self.done("v", i, reads=rk, writes=wk)

    def memset(self, ap, val, wk, eng="v"):
        wk = tuple(wk)
        self.dep(eng, writes=wk)
        i = None if self.record else self.eng_obj(eng).memset(ap, val)
        self.done(eng, i, writes=wk)

    def load(self, out_ap, in_ap, wk, ch):
        wk = tuple(wk)
        self.dep("s", writes=wk)
        i = None
        if not self.record:
            i = self.nc.sync.dma_start(out=out_ap, in_=in_ap)
        self.done("dma:" + ch, i, writes=wk)

    def store(self, out_ap, in_ap, rk, ch):
        rk = tuple(rk)
        self.dep("s", reads=rk)
        i = None
        if not self.record:
            i = self.nc.sync.dma_start(out=out_ap, in_=in_ap)
        self.done("dma:" + ch, i, reads=rk)

    def fence_channel(self, ch):
        tot = self.dcnt.get("dma:" + ch, 0)
        for k, (e, c) in list(self.last_w.items()):
            if e == "dma:" + ch:
                self.last_w[k] = (e, tot)

    def raw(self, fn, *a, **k):
        if not self.record:
            fn(*a, **k)

    def wait_all_dma(self):
        if not self.record:
            for ch, n in sorted(self.dcnt.items()):
                self.nc.sync.wait_ge(self.sems[ch], 16 * n)


def host_constants():
    idx = np.arange(C)
    consts = {
        "ident": np.eye(C, dtype=np.float32),
        "tri_incl": (idx[:, None] <= idx[None, :]).astype(np.float32),
        "tri_rev": (idx[:, None] > idx[None, :]).astype(np.float32),
        "maskn_s": np.where(idx[:, None] < idx[None, :], 0.0, NBIG).astype(np.float32),
        "maskn_sT": np.where(idx[:, None] > idx[None, :], 0.0, NBIG).astype(np.float32),
        "maskn_i": np.where(idx[:, None] <= idx[None, :], 0.0, NBIG).astype(np.float32),
        "ones_col": np.ones((C, 1), np.float32),
        "ones_row": np.ones((1, C), np.float32),
        "qscale_row": np.full((1, C), 128.0 ** -0.5, np.float32),
    }
    eh = np.zeros((16, 16 * C), np.float32)
    for h in range(16):
        eh[h, h * C:(h + 1) * C] = 1.0
    consts["eh"] = eh
    return consts


def build_program(nt1=NT1, nt2=NT2, taps=False):
    nc = bass.Bass("TRN2", target_bir_lowering=False, debug=False)

    d_xfm = nc.dram_tensor("x_fm", [HID, T], F32, kind="ExternalInput")
    d_xown = nc.dram_tensor("x_own_fm", [HID, TOW], F32, kind="ExternalInput")
    d_wqkvz = nc.dram_tensor("wqkvz", [HID, QKVZ_L], F32, kind="ExternalInput")
    d_wb = nc.dram_tensor("wb", [HID, 16], F32, kind="ExternalInput")
    d_wa = nc.dram_tensor("wa", [HID, 16], F32, kind="ExternalInput")
    d_convw = nc.dram_tensor("convw", [C, 128], F32, kind="ExternalInput")
    d_dtb = nc.dram_tensor("dtb", [16, 1], F32, kind="ExternalInput")
    d_alog = nc.dram_tensor("alog", [16, 1], F32, kind="ExternalInput")
    d_nw = nc.dram_tensor("nw", [1, 128], F32, kind="ExternalInput")
    d_wout = nc.dram_tensor("wout", [2 * HID, HID], F32, kind="ExternalInput")
    d_gate = nc.dram_tensor("gate", [HID, 8], F32, kind="ExternalInput")
    d_up = nc.dram_tensor("up", [HID, INTER], F32, kind="ExternalInput")
    d_down = nc.dram_tensor("down", [INTER, HID], F32, kind="ExternalInput")
    d_expert = nc.dram_tensor("expert", [HID, HID], F32, kind="ExternalInput")
    d_gam = nc.dram_tensor("gam", [C, 16], F32, kind="ExternalInput")
    d_bet = nc.dram_tensor("bet", [C, 16], F32, kind="ExternalInput")
    d_tb = nc.dram_tensor("tokbase", [1, 1], mybir.dt.int32, kind="ExternalInput")
    hc = host_constants()
    dc = {k: nc.dram_tensor("c_" + k, list(v.shape), F32, kind="ExternalInput")
          for k, v in hc.items()}

    d_out = nc.dram_tensor("out_fm", [HID, TOW], F32, kind="ExternalOutput")
    d_oga = nc.dram_tensor("og_a2a", [HID, T], F32)
    d_ogf = nc.dram_tensor("og_full", [2 * HID, T], F32)
    d_tap = None
    if taps:
        d_tap = nc.dram_tensor("tap", [2 * HID, T], F32, kind="ExternalOutput")

    es = contextlib.ExitStack()
    with es:
        def sb(name, p, f):
            return es.enter_context(nc.sbuf_tensor(name, [p, f], F32))

        cs = {k: sb("cs_" + k, *hc[k].shape) for k in hc}
        t_wb = sb("t_wb", 128, 16 * 16)
        t_wa = sb("t_wa", 128, 16 * 16)
        t_convw = sb("t_convw", C, 128)
        t_dtb = sb("t_dtb", 16, 1)
        t_negea = sb("t_negea", 16, 1)
        t_nwrow = sb("t_nwrow", 1, 128)
        t_nwb = sb("t_nwb", C, 128)
        t_gam = sb("t_gam", C, 16)
        t_bet = sb("t_bet", C, 16)
        t_state = sb("t_state", C, NVH_L * DV)
        t_tail = sb("t_tail", C, 32 * 4)
        t_tb = es.enter_context(nc.sbuf_tensor("t_tb", [1, 1], mybir.dt.int32))
        ARENA_F = 46432
        arena = es.enter_context(nc.sbuf_tensor("arena", [128, ARENA_F], F32))

        class Carver:
            def __init__(self):
                self.off = 0

            def take(self, p, f):
                o = (self.off + 15) // 16 * 16
                self.off = o + f
                assert self.off <= ARENA_F, f"arena overflow {self.off}"
                return arena[0:p, o:o + f]

        c1 = Carver()
        t_xT = c1.take(C, 16 * TT1)
        t_wsl = c1.take(C, 3 * 1024)
        t_mix = c1.take(C, 32 * (TT1 + 4))
        t_qkv = c1.take(C, 32 * TT1)
        t_z = c1.take(C, 16 * TT1)
        t_cv = [c1.take(C, TT1) for i in range(4)]
        t_ba = c1.take(16, 2 * TT1)
        t_beta = c1.take(16, TT1)
        t_g = c1.take(16, TT1)
        t_sp = c1.take(16, TT1)
        t_lb = c1.take(16, TT1)
        t_cum = c1.take(C, 16)
        t_cumT = c1.take(16, C)
        t_lba = c1.take(C, 16)
        t_lbaT = c1.take(16, C)
        t_dect = c1.take(C, 16)
        t_bacol = c1.take(C, 16)
        t_btok = c1.take(C, 16)
        t_gtok = c1.take(C, 16)
        t_kh = c1.take(C, NKH_L * C)
        t_qh = c1.take(C, NKH_L * C)
        t_sq = c1.take(C, C)
        t_rstd = c1.take(1, 2 * C)
        RV = 4
        t_DB = [c1.take(C, C) for i in range(RV)]
        t_Dq = [c1.take(C, C) for i in range(RV)]
        t_DBz = [c1.take(C, C) for i in range(RV)]
        t_bA = [c1.take(C, C) for i in range(RV)]
        t_Y = [[c1.take(C, C) for j in range(2)] for i in range(RV)]
        t_Z = [[c1.take(C, C) for j in range(2)] for i in range(RV)]
        t_U = [[c1.take(C, C) for j in range(2)] for i in range(RV)]
        t_P = [c1.take(C, C) for i in range(RV)]
        t_Qa = [c1.take(C, C) for i in range(RV)]
        t_G0 = [c1.take(C, C) for i in range(2)]
        t_G1 = [c1.take(C, C) for i in range(2)]
        t_Kt = [c1.take(C, C) for i in range(2)]
        t_r1 = [c1.take(C, C) for i in range(RV)]
        t_r2 = [c1.take(C, C) for i in range(RV)]
        t_sz = [c1.take(C, C) for i in range(RV)]
        t_og0 = [c1.take(C, C) for i in range(RV)]
        t_msq0 = c1.take(C, C)
        t_ms = [c1.take(C, 1) for i in range(RV)]
        t_rst = [c1.take(C, 1) for i in range(RV)]
        t_ogt = c1.take(C, 16 * TT1)
        c2v = Carver()
        t_x1 = c2v.take(C, 16 * TT2)
        t_y2 = c2v.take(C, 16 * TT2)
        t_h2 = c2v.take(C, 8 * TT2)
        t_og2 = c2v.take(C, 2 * 8 * TT2)
        t_w2 = c2v.take(C, 4 * 1024)
        t_xr = c2v.take(C, 8 * TT2)
        t_sqa = c2v.take(C, 2 * TT2)
        t_st1 = c2v.take(1, TT2)
        t_st2 = c2v.take(1, TT2)
        t_st3 = c2v.take(1, TT2)
        t_lg = c2v.take(8, TT2)
        t_ltok = [c2v.take(C, 8) for i in range(4)]
        t_msk = [c2v.take(C, 8) for i in range(4)]
        t_m1 = [c2v.take(C, 1) for i in range(4)]
        t_m2c = [c2v.take(C, 1) for i in range(4)]
        t_cnt = [c2v.take(C, 1) for i in range(4)]
        t_d1 = [c2v.take(C, 1) for i in range(4)]
        t_t2r = c2v.take(1, TT2)
        t_t2b = c2v.take(C, TT2)
        t_o2 = c2v.take(C, 4 * TT2)

        pb = [es.enter_context(nc.psum_tensor(f"pb{i}", [C, 512], F32))
              for i in range(8)]
        PS = [("PS", i) for i in range(8)]

        sems = {}
        for e in ("t", "v", "a", "s", "p", "cc"):
            sems[e] = es.enter_context(nc.semaphore("sem_" + e))
        DMACH = ("pro", "xT", "wsl0", "wsl1", "wsl2", "wsl3", "ogst",
                 "w20", "w21", "w22", "w23", "og20", "og21", "xr", "outst")
        for ch in DMACH:
            sems["dma:" + ch] = es.enter_context(nc.semaphore("semd_" + ch))

        def prologue(tr):
            for k in dc:
                tr.load(cs[k][:], dc[k][:, :], wk=[("c", k)], ch="pro")
            tr.load(t_convw[:], d_convw[:, :], wk=[("convw",)], ch="pro")
            tr.load(t_dtb[:], d_dtb[:, :], wk=[("dtb",)], ch="pro")
            tr.load(t_negea[:], d_alog[:, :], wk=[("negea",)], ch="pro")
            tr.load(t_gam[:], d_gam[:, :], wk=[("gam",)], ch="pro")
            tr.load(t_bet[:], d_bet[:, :], wk=[("bet",)], ch="pro")
            for kc in range(16):
                tr.load(t_wb[:, kc * 16:(kc + 1) * 16],
                        d_wb[kc * 128:(kc + 1) * 128, :], wk=[("wbk", kc)],
                        ch="pro")
                tr.load(t_wa[:, kc * 16:(kc + 1) * 16],
                        d_wa[kc * 128:(kc + 1) * 128, :], wk=[("wak", kc)],
                        ch="pro")
            tr.load(t_nwrow[:], d_nw[:, :], wk=[("nwrow",)], ch="pro")
            tr.load(t_tb[:], d_tb[:, :], wk=[("tb",)], ch="pro")
            tr.fence_channel("pro")
            tr.mm(pb[0][:, 0:128], cs["ones_row"][:], t_nwrow[:],
                  rk=[("c", "ones_row"), ("nwrow",)], wk=PS[0])
            tr.act(t_nwb[:], pb[0][:, 0:128], AF.Copy, rk=[PS[0]], wk=[("nwb",)])
            tr.act(t_negea[:], t_negea[:], AF.Exp, rk=[("negea",)], wk=[("negea",)])
            tr.vts(t_negea[:], t_negea[:], -1.0, None, ALU.mult,
                   rk=[("negea",)], wk=[("negea",)])
            tr.memset(t_state[:], 0.0, wk=[("sv", v) for v in range(16)])
            tr.memset(t_tail[:], 0.0, wk=[("tail", cc_) for cc_ in range(32)])

        def p1_body(tr, it):
            tr.it_regs = it["regs"]
            ogrow = ogcol = 0
            rx_ = it["regs"]["rx"]
            rtmp_ = it["regs"]["tmp"]
            for half in range(2):
                if half == 0:
                    off_reg = rx_
                else:
                    tr.raw(nc.sync.reg_add, rtmp_, rx_, 8 * 128 * T)
                    off_reg = rtmp_
                ap = bass.AP(d_xfm, off_reg,
                             [[T, 128], [128 * T, 8], [1, TT1]])
                tr.load(t_xT[:, half * 8 * TT1:(half + 1) * 8 * TT1], ap,
                        wk=[("xT", fc) for fc in
                            range(half * 8, half * 8 + 8)], ch="xT")
            tr.fence_channel("xT")
            for grp in range(6):
                for kc in range(16):
                    wb_ = kc % 3
                    slab = t_wsl[:, wb_ * 1024:wb_ * 1024 + 1024]
                    tr.load(slab,
                            d_wqkvz[kc * 128:(kc + 1) * 128,
                                    grp * 1024:(grp + 1) * 1024],
                            wk=[("wsl", wb_)], ch=f"wsl{wb_}")
                    for m in range(8):
                        tr.mm(pb[m][:, 0:TT1],
                              slab[:, m * 128:(m + 1) * 128],
                              t_xT[:, kc * TT1:(kc + 1) * TT1],
                              rk=[("wsl", wb_), ("xT", kc)],
                              wk=PS[m], start=(kc == 0), stop=(kc == 15))
                for m in range(8):
                    mc = grp * 8 + m
                    kh, r = divmod(mc, 6)
                    if r == 4 or r == 5:
                        zc = 2 * kh + (r - 4)
                        tr.act(t_z[:, zc * TT1:(zc + 1) * TT1], pb[m][:, 0:TT1],
                               AF.Copy, rk=[PS[m]], wk=[("z", zc)])
                        continue
                    if r == 0:
                        cc_ = kh
                    elif r == 1:
                        cc_ = 8 + kh
                    else:
                        cc_ = 16 + 2 * kh + (r - 2)
                    base = cc_ * (TT1 + 4)
                    tr.vcopy(t_mix[:, base:base + 3],
                             t_tail[:, cc_ * 4:cc_ * 4 + 3],
                             rk=[("tail", cc_)], wk=[("mixh", cc_)])
                    tr.act(t_mix[:, base + 3:base + 3 + TT1], pb[m][:, 0:TT1],
                           AF.Copy, rk=[PS[m]], wk=[("mix", cc_)])
                    tr.vcopy(t_tail[:, cc_ * 4:cc_ * 4 + 3],
                             t_mix[:, base + TT1:base + TT1 + 3],
                             rk=[("mix", cc_)], wk=[("tail", cc_)])
            for outc, wsb, keyn in ((0, t_wb, "wbk"), (1, t_wa, "wak")):
                for kc in range(16):
                    tr.mm(pb[0][0:16, 256 * outc:256 * outc + TT1],
                          wsb[:, kc * 16:(kc + 1) * 16],
                          t_xT[:, kc * TT1:(kc + 1) * TT1],
                          rk=[(keyn, kc), ("xT", kc)],
                          wk=PS[0], start=(kc == 0), stop=(kc == 15))
                tr.act(t_ba[:, outc * TT1:(outc + 1) * TT1],
                       pb[0][0:16, 256 * outc:256 * outc + TT1], AF.Copy,
                       rk=[PS[0]], wk=[("bax", outc)])
            tr.act(t_beta[:], t_ba[:, 0:TT1], AF.Sigmoid,
                   rk=[("bax", 0)], wk=[("beta",)])
            tr.vts(t_sp[:], t_ba[:, TT1:2 * TT1], t_dtb[:, 0:1], None, ALU.add,
                   rk=[("bax", 1), ("dtb",)], wk=[("spa",)])
            tr.vts(t_g[:], t_sp[:], 0.0, None, ALU.max,
                   rk=[("spa",)], wk=[("gmax",)])
            tr.act(t_sp[:], t_sp[:], AF.Abs, rk=[("spa",)], wk=[("spa",)])
            tr.act(t_sp[:], t_sp[:], AF.Exp, rk=[("spa",)], wk=[("spa",)],
                   scale=-1.0)
            tr.vts(t_sp[:], t_sp[:], 1.0, None, ALU.add,
                   rk=[("spa",)], wk=[("spa",)])
            tr.act(t_sp[:], t_sp[:], AF.Ln, rk=[("spa",)], wk=[("spa",)])
            tr.vtt(t_g[:], t_g[:], t_sp[:], ALU.add,
                   rk=[("gmax",), ("spa",)], wk=[("gmax",)])
            tr.vts(t_g[:], t_g[:], t_negea[:, 0:1], None, ALU.mult,
                   rk=[("gmax",), ("negea",)], wk=[("g",)])
            tr.act(t_lb[:], t_beta[:], AF.Ln, rk=[("beta",)], wk=[("lb",)])
            for cc_ in range(32):
                base = cc_ * (TT1 + 4)
                mixk, hk = ("mix", cc_), ("mixh", cc_)
                wcol = lambda j: t_convw[:, cc_ * 4 + j:cc_ * 4 + j + 1]
                tr.vts(t_cv[0][:], t_mix[:, base:base + TT1], wcol(0), None,
                       ALU.mult, rk=[mixk, hk, ("convw",)], wk=[("cv", 0)])
                tr.vts(t_cv[1][:], t_mix[:, base + 1:base + 1 + TT1], wcol(1),
                       None, ALU.mult, rk=[mixk, hk, ("convw",)],
                       wk=[("cv", 1)], eng="p")
                tr.vts(t_cv[2][:], t_mix[:, base + 2:base + 2 + TT1], wcol(2),
                       None, ALU.mult, rk=[mixk, hk, ("convw",)],
                       wk=[("cv", 2)])
                tr.vts(t_cv[3][:], t_mix[:, base + 3:base + 3 + TT1], wcol(3),
                       None, ALU.mult, rk=[mixk, ("convw",)],
                       wk=[("cv", 3)], eng="p")
                tr.vtt(t_cv[0][:], t_cv[0][:], t_cv[1][:], ALU.add,
                       rk=[("cv", 0), ("cv", 1)], wk=[("cv", 0)])
                tr.vtt(t_cv[2][:], t_cv[2][:], t_cv[3][:], ALU.add,
                       rk=[("cv", 2), ("cv", 3)], wk=[("cv", 2)], eng="p")
                tr.vtt(t_cv[0][:], t_cv[0][:], t_cv[2][:], ALU.add,
                       rk=[("cv", 0), ("cv", 2)], wk=[("cv", 0)])
                tr.act(t_qkv[:, cc_ * TT1:(cc_ + 1) * TT1], t_cv[0][:],
                       AF.Silu, rk=[("cv", 0)], wk=[("qkv", cc_)])
            for ci in range(TT1 // C):
                scan_chunk(tr, ci, ogrow, ogcol)

        def scan_chunk(tr, ci, ogrow, ogcol):
            t0 = ci * C
            tr.transpose(pb[0][:, 0:16], t_g[:, t0:t0 + C],
                         cs["ident"][0:16, 0:16],
                         rk=[("g",), ("c", "ident")], wk=PS[0])
            tr.act(t_gtok[:], pb[0][:, 0:16], AF.Copy, rk=[PS[0]],
                   wk=[("gtok",)])
            tr.transpose(pb[1][:, 0:16], t_beta[:, t0:t0 + C],
                         cs["ident"][0:16, 0:16],
                         rk=[("beta",), ("c", "ident")], wk=PS[1])
            tr.act(t_btok[:], pb[1][:, 0:16], AF.Copy, rk=[PS[1]],
                   wk=[("btok",)])
            tr.transpose(pb[2][:, 0:16], t_lb[:, t0:t0 + C],
                         cs["ident"][0:16, 0:16],
                         rk=[("lb",), ("c", "ident")], wk=PS[2])
            tr.act(t_lba[:], pb[2][:, 0:16], AF.Copy, rk=[PS[2]],
                   wk=[("lbtok",)])
            tr.mm(pb[3][:, 0:16], cs["tri_incl"][:], t_gtok[:],
                  rk=[("c", "tri_incl"), ("gtok",)], wk=PS[3])
            tr.act(t_cum[:], pb[3][:, 0:16], AF.Copy, rk=[PS[3]],
                   wk=[("cum",)])
            tr.mm(pb[4][:, 0:16], cs["tri_rev"][:], t_gtok[:],
                  rk=[("c", "tri_rev"), ("gtok",)], wk=PS[4])
            tr.act(t_dect[:], pb[4][:, 0:16], AF.Exp, rk=[PS[4]],
                   wk=[("dect",)])
            tr.vtt(t_lba[:], t_lba[:], t_cum[:], ALU.add,
                   rk=[("lbtok",), ("cum",)], wk=[("lba",)])
            tr.act(t_bacol[:], t_lba[:], AF.Exp, rk=[("lba",)], wk=[("bacol",)])
            tr.transpose(pb[5][0:16, 0:128], t_cum[:], cs["ident"][:, :],
                         rk=[("cum",), ("c", "ident")], wk=PS[5])
            tr.act(t_cumT[0:16, :], pb[5][0:16, 0:128], AF.Copy,
                   rk=[PS[5]], wk=[("cumT",)])
            tr.transpose(pb[6][0:16, 0:128], t_lba[:], cs["ident"][:, :],
                         rk=[("lba",), ("c", "ident")], wk=PS[6])
            tr.act(t_lbaT[0:16, :], pb[6][0:16, 0:128], AF.Copy,
                   rk=[PS[6]], wk=[("lbaT",)])
            for kh in range(NKH_L):
                for qk in range(2):
                    cc_ = kh if qk == 0 else 8 + kh
                    src = t_qkv[:, cc_ * TT1 + t0: cc_ * TT1 + t0 + C]
                    dst = t_qh if qk == 0 else t_kh
                    dk = ("qh", kh) if qk == 0 else ("kh", kh)
                    tr.act(t_sq[:], src, AF.Square, rk=[("qkv", cc_)],
                           wk=[("sq",)])
                    tr.mm(pb[7][0:1, qk * C:(qk + 1) * C], cs["ones_col"][:],
                          t_sq[:], rk=[("c", "ones_col"), ("sq",)], wk=PS[7])
                    tr.vts(t_rstd[0:1, qk * C:(qk + 1) * C],
                           pb[7][0:1, qk * C:(qk + 1) * C], EPS_L2, None,
                           ALU.add, rk=[PS[7]], wk=[("rstl", qk)])
                    tr.act(t_rstd[0:1, qk * C:(qk + 1) * C],
                           t_rstd[0:1, qk * C:(qk + 1) * C], AF.Ln,
                           rk=[("rstl", qk)], wk=[("rstl", qk)])
                    tr.act(t_rstd[0:1, qk * C:(qk + 1) * C],
                           t_rstd[0:1, qk * C:(qk + 1) * C], AF.Exp,
                           rk=[("rstl", qk)], wk=[("rstl", qk)], scale=-0.5)
                    tr.mm(pb[0][:, qk * C:(qk + 1) * C],
                          cs["qscale_row" if qk == 0 else "ones_row"][:],
                          t_rstd[0:1, qk * C:(qk + 1) * C],
                          rk=[("c", "ones_row"), ("rstl", qk)], wk=PS[0])
                    tr.vtt(dst[:, kh * C:(kh + 1) * C], src,
                           pb[0][:, qk * C:(qk + 1) * C], ALU.mult,
                           rk=[("qkv", cc_), PS[0]], wk=[dk])
            for rnd in range(4):
                scan_round(tr, rnd, ci, ogrow, ogcol)
            if ci == TT1 // C - 1:
                rtmp_ = tr.it_regs["tmp"]
                for half in range(2):
                    tr.raw(nc.sync.reg_add, rtmp_, tr.it_regs["rx"],
                           half * 8 * 128 * T)
                    dst = bass.AP(d_oga, rtmp_,
                                  [[T, 128], [128 * T, 8], [1, TT1]])
                    tr.store(dst,
                             t_ogt[:, half * 8 * TT1:(half + 1) * 8 * TT1],
                             rk=[("ogt", v, cc)
                                 for v in range(half * 8, half * 8 + 8)
                                 for cc in (0, 1)], ch="ogst")

        def scan_round(tr, rnd, ci, ogrow, ogcol):
            vhs = [rnd * 4 + j for j in range(4)]
            khs = [rnd * 2, rnd * 2 + 1]
            t0 = ci * C
            for j, kh in enumerate(khs):
                ksl = t_kh[:, kh * C:(kh + 1) * C]
                qsl = t_qh[:, kh * C:(kh + 1) * C]
                tr.mm(pb[0][:, j * C:(j + 1) * C], ksl, ksl,
                      rk=[("kh", kh)], wk=PS[0])
                tr.mm(pb[0][:, 256 + j * C:256 + (j + 1) * C], ksl, qsl,
                      rk=[("kh", kh), ("qh", kh)], wk=PS[0])
                tr.act(t_G0[j][:], pb[0][:, j * C:(j + 1) * C], AF.Copy,
                       rk=[PS[0]], wk=[("G0", j)], scale=-1.0)
                tr.act(t_G1[j][:], pb[0][:, 256 + j * C:256 + (j + 1) * C],
                       AF.Copy, rk=[PS[0]], wk=[("G1", j)])
                tr.transpose(pb[1][:, j * C:(j + 1) * C], ksl, cs["ident"][:],
                             rk=[("kh", kh), ("c", "ident")], wk=PS[1])
                tr.act(t_Kt[j][:], pb[1][:, j * C:(j + 1) * C], AF.Copy,
                       rk=[PS[1]], wk=[("Kt", j)])
            for j, v in enumerate(vhs):
                eh = cs["eh"][:, v * C:(v + 1) * C]
                bnk = 2 + j
                tr.mm(pb[bnk][:, 0:C], eh, t_lbaT[0:16, :],
                      rk=[("c", "eh"), ("lbaT",)], wk=PS[bnk])
                tr.mm(pb[bnk][:, C:2 * C], eh, t_cumT[0:16, :],
                      rk=[("c", "eh"), ("cumT",)], wk=PS[bnk])
                tr.vts(t_DB[j][:], pb[bnk][:, 0:C], t_cum[:, v:v + 1], None,
                       ALU.subtract, rk=[PS[bnk], ("cum",)], wk=[("DB", j)])
                tr.vtt(t_DB[j][:], t_DB[j][:], cs["maskn_s"][:], ALU.add,
                       rk=[("DB", j), ("c", "maskn_s")], wk=[("DB", j)],
                       eng="p")
                tr.act(t_DB[j][:], t_DB[j][:], AF.Exp, rk=[("DB", j)],
                       wk=[("DB", j)])
                tr.vts(t_Dq[j][:], pb[bnk][:, C:2 * C], t_cum[:, v:v + 1],
                       None, ALU.subtract, rk=[PS[bnk], ("cum",)],
                       wk=[("Dq", j)])
                tr.vtt(t_Dq[j][:], t_Dq[j][:], cs["maskn_i"][:], ALU.add,
                       rk=[("Dq", j), ("c", "maskn_i")], wk=[("Dq", j)],
                       eng="p")
                tr.act(t_Dq[j][:], t_Dq[j][:], AF.Exp, rk=[("Dq", j)],
                       wk=[("Dq", j)])
                tr.vts(t_DBz[j][:], pb[bnk][:, C:2 * C], t_lba[:, v:v + 1],
                       None, ALU.subtract, rk=[PS[bnk], ("lba",)],
                       wk=[("DBz", j)])
                tr.vts(t_DBz[j][:], t_DBz[j][:], -1.0, None, ALU.mult,
                       rk=[("DBz", j)], wk=[("DBz", j)])
                tr.vtt(t_DBz[j][:], t_DBz[j][:], cs["maskn_sT"][:], ALU.add,
                       rk=[("DBz", j), ("c", "maskn_sT")], wk=[("DBz", j)],
                       eng="p")
                tr.act(t_DBz[j][:], t_DBz[j][:], AF.Exp, rk=[("DBz", j)],
                       wk=[("DBz", j)])
                tr.act(t_bA[j][:], pb[bnk][:, C:2 * C], AF.Exp,
                       rk=[PS[bnk]], wk=[("bA", j)])
                kj = j // 2
                tr.vtt(t_Y[j][0][:], t_DB[j][:], t_G0[kj][:], ALU.mult,
                       rk=[("DB", j), ("G0", kj)], wk=[("Y", j, 0)], eng="p")
                tr.vtt(t_Z[j][0][:], t_DBz[j][:], t_G0[kj][:], ALU.mult,
                       rk=[("DBz", j), ("G0", kj)], wk=[("Z", j, 0)], eng="p")
                tr.vtt(t_P[j][:], t_Dq[j][:], t_G1[kj][:], ALU.mult,
                       rk=[("Dq", j), ("G1", kj)], wk=[("P", j)], eng="p")
                tr.vtt(t_Qa[j][:], t_qh[:, (v // 2) * C:(v // 2 + 1) * C],
                       t_bA[j][:], ALU.mult,
                       rk=[("qh", v // 2), ("bA", j)], wk=[("Qa", j)], eng="p")
            for j, kh in enumerate(khs):
                tr.mm(pb[6][:, j * 256:(j + 1) * 256],
                      t_kh[:, kh * C:(kh + 1) * C],
                      t_state[:, kh * 256:(kh + 1) * 256],
                      rk=[("kh", kh), ("sv", 2 * kh), ("sv", 2 * kh + 1)],
                      wk=PS[6])
            for j, v in enumerate(vhs):
                vc = 16 + v
                tr.transpose(pb[2 + j][:, 2 * C:3 * C],
                             t_qkv[:, vc * TT1 + t0: vc * TT1 + t0 + C],
                             cs["ident"][:],
                             rk=[("qkv", vc), ("c", "ident")], wk=PS[2 + j])
                tr.vts(t_r1[j][:], pb[2 + j][:, 2 * C:3 * C],
                       t_btok[:, v:v + 1], None, ALU.mult,
                       rk=[PS[2 + j], ("btok",)], wk=[("r1", j)])
                kj = j // 2
                tr.vts(t_r2[j][:],
                       pb[6][:, kj * 256 + (v % 2) * C:
                             kj * 256 + (v % 2) * C + C],
                       t_bacol[:, v:v + 1], None, ALU.mult,
                       rk=[PS[6], ("bacol",)], wk=[("r2", j)])
                tr.vtt(t_U[j][0][:], t_r1[j][:], t_r2[j][:], ALU.subtract,
                       rk=[("r1", j), ("r2", j)], wk=[("U", j, 0)])
            for lev in range(7):
                yb = lev % 2
                ub = lev % 2
                for j in range(4):
                    tr.mm(pb[j][:, 0:C], t_Y[j][yb][:], t_U[j][ub][:],
                          rk=[("Y", j, yb), ("U", j, ub)], wk=PS[j])
                for j in range(4):
                    tr.vtt(t_U[j][1 - ub][:], pb[j][:, 0:C],
                           t_U[j][ub][:], ALU.add,
                           rk=[PS[j], ("U", j, ub)], wk=[("U", j, 1 - ub)])
                if lev < 6:
                    for j in range(4):
                        tr.mm(pb[4 + j][:, 0:C],
                              t_Z[j][yb][:], t_Y[j][yb][:],
                              rk=[("Z", j, yb), ("Y", j, yb)], wk=PS[4 + j])
                    for j in range(4):
                        tr.act(t_Y[j][1 - yb][:], pb[4 + j][:, 0:C],
                               AF.Copy, rk=[PS[4 + j]], wk=[("Y", j, 1 - yb)])
                    if lev < 5:
                        for j in range(4):
                            tr.mm(pb[j][:, C:2 * C],
                                  t_Y[j][yb][:], t_Z[j][yb][:],
                                  rk=[("Y", j, yb), ("Z", j, yb)], wk=PS[j])
                        for j in range(4):
                            tr.act(t_Z[j][1 - yb][:],
                                   pb[j][:, C:2 * C], AF.Copy,
                                   rk=[PS[j]], wk=[("Z", j, 1 - yb)])
            UF = 1
            for j, v in enumerate(vhs):
                tr.transpose(pb[4 + j][:, 0:C],
                             t_z[:, v * TT1 + t0:v * TT1 + t0 + C],
                             cs["ident"][:],
                             rk=[("z", v), ("c", "ident")], wk=PS[4 + j])
            for j, v in enumerate(vhs):
                tr.act(t_sz[j][:], pb[4 + j][:, 0:C], AF.Silu,
                       rk=[PS[4 + j]], wk=[("szj", j)])
            for j, v in enumerate(vhs):
                tr.mm(pb[j][:, 0:C], t_Qa[j][:],
                      t_state[:, v * C:(v + 1) * C],
                      rk=[("Qa", j), ("sv", v)], wk=PS[j],
                      start=True, stop=False)
                tr.mm(pb[j][:, 0:C], t_P[j][:], t_U[j][UF][:],
                      rk=[("P", j), ("U", j, UF)], wk=PS[j],
                      start=False, stop=True)
            for j, v in enumerate(vhs):
                tr.vts(t_r1[j][:], t_Kt[j // 2][:], t_dect[:, v:v + 1], None,
                       ALU.mult, rk=[("Kt", j // 2), ("dect",)],
                       wk=[("r1", j)])
                tr.mm(pb[4 + j][:, 0:C], t_r1[j][:], t_U[j][UF][:],
                      rk=[("r1", j), ("U", j, UF)], wk=PS[4 + j])
                tr.vts(t_r2[j][:], t_state[:, v * C:(v + 1) * C],
                       t_bA[j][:, C - 1:C], None, ALU.mult,
                       rk=[("sv", v), ("bA", j)], wk=[("r2", j)])
                tr.vtt(t_state[:, v * C:(v + 1) * C],
                       pb[4 + j][:, 0:C], t_r2[j][:], ALU.add,
                       rk=[PS[4 + j], ("r2", j)], wk=[("sv", v)])
            for j, v in enumerate(vhs):
                tr.vtt(t_og0[j][:], pb[j][:, 0:C], t_sz[j][:],
                       ALU.mult, rk=[PS[j], ("szj", j)], wk=[("og0", j)])
                tr.act(t_msq0[:], t_og0[j][:], AF.Square,
                       rk=[("og0", j)], wk=[("msq",), ("ms", j)],
                       accum_out=t_ms[j][:])
                tr.vts(t_rst[j][:], t_ms[j][:], 1.0 / 128.0, EPS_RMS,
                       ALU.mult, rk=[("ms", j)], wk=[("rst", j)],
                       op1=ALU.add)
                tr.act(t_rst[j][:], t_rst[j][:], AF.Ln, rk=[("rst", j)],
                       wk=[("rst", j)])
                tr.act(t_rst[j][:], t_rst[j][:], AF.Exp, rk=[("rst", j)],
                       wk=[("rst", j)], scale=-0.5)
                tr.vts(t_og0[j][:], t_og0[j][:], t_rst[j][:, 0:1], None,
                       ALU.mult, rk=[("og0", j), ("rst", j)], wk=[("og0", j)])
                tr.vtt(t_og0[j][:], t_og0[j][:], t_nwb[:], ALU.mult,
                       rk=[("og0", j), ("nwb",)], wk=[("og0", j)], eng="p")
                tr.transpose(pb[j][:, C:2 * C], t_og0[j][:],
                             cs["ident"][:],
                             rk=[("og0", j), ("c", "ident")], wk=PS[j])
                tr.act(t_ogt[:, v * TT1 + t0:v * TT1 + t0 + C],
                       pb[j][:, C:2 * C], AF.Copy,
                       rk=[PS[j]], wk=[("ogt", v, ci)])


        def ln_fm(tr, src, dst, sk, dk, gamma=None, beta=None, store_c2=None):
            for mc in range(16):
                tr.mm(pb[0][0:1, 0:TT2], cs["ones_col"][:],
                      src[:, mc * TT2:(mc + 1) * TT2],
                      rk=[("c", "ones_col"), (sk, mc)], wk=PS[0],
                      start=(mc == 0), stop=(mc == 15))
            for mc in range(16):
                sq = t_sqa[:, (mc % 2) * TT2:(mc % 2) * TT2 + TT2]
                tr.act(sq, src[:, mc * TT2:(mc + 1) * TT2], AF.Square,
                       rk=[(sk, mc)], wk=[("sqa", mc % 2)])
                tr.mm(pb[1][0:1, 0:TT2], cs["ones_col"][:], sq,
                      rk=[("c", "ones_col"), ("sqa", mc % 2)], wk=PS[1],
                      start=(mc == 0), stop=(mc == 15))
            tr.vts(t_st1[:], pb[0][0:1, 0:TT2], 1.0 / HID, None, ALU.mult,
                   rk=[PS[0]], wk=[("mu",)])
            tr.vts(t_st2[:], pb[1][0:1, 0:TT2], 1.0 / HID, None, ALU.mult,
                   rk=[PS[1]], wk=[("var",)])
            tr.vtt(t_st3[:], t_st1[:], t_st1[:], ALU.mult,
                   rk=[("mu",)], wk=[("mu2",)])
            tr.vtt(t_st2[:], t_st2[:], t_st3[:], ALU.subtract,
                   rk=[("var",), ("mu2",)], wk=[("var",)])
            tr.vts(t_st2[:], t_st2[:], EPS_LN, None, ALU.add,
                   rk=[("var",)], wk=[("var",)])
            tr.act(t_st2[:], t_st2[:], AF.Ln, rk=[("var",)], wk=[("var",)])
            tr.act(t_st2[:], t_st2[:], AF.Exp, rk=[("var",)], wk=[("var",)],
                   scale=-0.5)
            tr.mm(pb[2][:, 0:TT2], cs["ones_row"][:], t_st1[:],
                  rk=[("c", "ones_row"), ("mu",)], wk=PS[2])
            tr.mm(pb[3][:, 0:TT2], cs["ones_row"][:], t_st2[:],
                  rk=[("c", "ones_row"), ("var",)], wk=PS[3])
            for mc in range(16):
                tmp = t_sqa[:, (mc % 2) * TT2:(mc % 2) * TT2 + TT2]
                tr.vtt(tmp, src[:, mc * TT2:(mc + 1) * TT2], pb[2][:, 0:TT2],
                       ALU.subtract, rk=[(sk, mc), PS[2]],
                       wk=[("sqa", mc % 2)])
                if store_c2 is not None:
                    ob = t_o2[:, (mc % 4) * TT2:(mc % 4) * TT2 + TT2]
                    tr.vtt(ob, tmp, pb[3][:, 0:TT2], ALU.mult,
                           rk=[("sqa", mc % 2), PS[3]], wk=[("o2", mc % 4)])
                    if mc % 4 == 3:
                        rg = tr.it_regs2
                        tr.raw(nc.sync.reg_add, rg["tmp2"], rg["rc2"],
                               (mc - 3) * 128 * TOW)
                        dap = bass.AP(d_out, rg["tmp2"],
                                      [[TOW, 128], [128 * TOW, 4], [1, TT2]])
                        tr.store(dap, t_o2[:, 0:4 * TT2],
                                 rk=[("o2", i) for i in range(4)], ch="outst")
                elif gamma is not None:
                    tr.vtt(tmp, tmp, pb[3][:, 0:TT2], ALU.mult,
                           rk=[("sqa", mc % 2), PS[3]], wk=[("sqa", mc % 2)])
                    tr.vts(dst[:, mc * TT2:(mc + 1) * TT2], tmp,
                           gamma[:, mc:mc + 1], beta[:, mc:mc + 1], ALU.mult,
                           rk=[("sqa", mc % 2), ("gam",), ("bet",)],
                           wk=[(dk, mc)], op1=ALU.add)
                else:
                    tr.vtt(dst[:, mc * TT2:(mc + 1) * TT2], tmp,
                           pb[3][:, 0:TT2], ALU.mult,
                           rk=[("sqa", mc % 2), PS[3]], wk=[(dk, mc)])

        def p2_body(tr, it):
            rg = it["regs"]
            tr.it_regs2 = rg
            rc2_, rtog_, rt2_ = rg["rc2"], rg["rtog"], rg["tmp2"]
            for mh in range(2):
                tr.raw(nc.sync.reg_add, rt2_, rc2_, mh * 8 * 128 * TOW)
                xap = bass.AP(d_xown, rt2_,
                              [[TOW, 128], [128 * TOW, 8], [1, TT2]])
                tr.load(t_xr[:, 0:8 * TT2], xap,
                        wk=[("xr", i) for i in range(8)], ch="xr")
                for s in range(4):
                    sb_ = s % 2
                    tr.raw(nc.sync.reg_add, rt2_, rtog_,
                           s * 8 * 128 * T)
                    oap = bass.AP(d_ogf, rt2_,
                                  [[T, 128], [128 * T, 8], [1, TT2]])
                    strip = t_og2[:, sb_ * 8 * TT2:(sb_ + 1) * 8 * TT2]
                    tr.load(strip, oap, wk=[("og2", sb_)], ch=f"og2{sb_}")
                    for kk in range(8):
                        k = s * 8 + kk
                        wb_ = k % 4
                        slab = t_w2[:, wb_ * 1024:wb_ * 1024 + 1024]
                        tr.load(slab, d_wout[k * 128:(k + 1) * 128,
                                             mh * 1024:(mh + 1) * 1024],
                                wk=[("w2", wb_)], ch=f"w2{wb_}")
                        for m in range(8):
                            tr.mm(pb[m][:, 0:TT2],
                                  slab[:, m * 128:(m + 1) * 128],
                                  strip[:, kk * TT2:(kk + 1) * TT2],
                                  rk=[("w2", wb_), ("og2", sb_)],
                                  wk=PS[m], start=(k == 0), stop=(k == 31))
                for m in range(8):
                    mc = mh * 8 + m
                    tr.vtt(t_y2[:, mc * TT2:(mc + 1) * TT2], pb[m][:, 0:TT2],
                           t_xr[:, m * TT2:(m + 1) * TT2], ALU.add,
                           rk=[PS[m], ("xr", m)], wk=[("y2", mc)])
            ln_fm(tr, t_y2, t_x1, "y2", "x1")
            for kc in range(16):
                wb_ = kc % 4
                slab = t_w2[:, wb_ * 1024:wb_ * 1024 + 8]
                tr.load(slab, d_gate[kc * 128:(kc + 1) * 128, :],
                        wk=[("w2", wb_)], ch=f"w2{wb_}")
                tr.mm(pb[4][0:8, 0:TT2], slab,
                      t_x1[:, kc * TT2:(kc + 1) * TT2],
                      rk=[("w2", wb_), ("x1", kc)], wk=PS[4],
                      start=(kc == 0), stop=(kc == 15))
            tr.act(t_lg[:], pb[4][0:8, 0:TT2], AF.Copy, rk=[PS[4]],
                   wk=[("lg",)])
            for tc in range(4):
                tr.transpose(pb[5][:, tc * 8:(tc + 1) * 8],
                             t_lg[:, tc * 128:(tc + 1) * 128],
                             cs["ident"][0:8, 0:8],
                             rk=[("lg",), ("c", "ident")], wk=PS[5])
                tr.act(t_ltok[tc][:], pb[5][:, tc * 8:(tc + 1) * 8], AF.Copy,
                       rk=[PS[5]], wk=[("ltok", tc)])
                tr.vts(t_msk[tc][:], t_ltok[tc][:], 0.0, None, ALU.add,
                       rk=[("ltok", tc)], wk=[("msk", tc), ("m1", tc)],
                       op1=ALU.max, accum_out=t_m1[tc][:])
                tr.vts(t_msk[tc][:], t_ltok[tc][:], t_m1[tc][:, 0:1], None,
                       ALU.is_ge, rk=[("ltok", tc), ("m1", tc)],
                       wk=[("msk", tc)])
                tr.vts(t_msk[tc][:], t_msk[tc][:], 0.0, None, ALU.add,
                       rk=[("msk", tc)], wk=[("msk", tc), ("cnt", tc)],
                       op1=ALU.add, accum_out=t_cnt[tc][:])
                tr.vts(t_msk[tc][:], t_msk[tc][:], 1e30, None, ALU.mult,
                       rk=[("msk", tc)], wk=[("msk", tc)])
                tr.vtt(t_msk[tc][:], t_ltok[tc][:], t_msk[tc][:],
                       ALU.subtract, rk=[("ltok", tc), ("msk", tc)],
                       wk=[("msk", tc)])
                tr.vts(t_msk[tc][:], t_msk[tc][:], 0.0, None, ALU.add,
                       rk=[("msk", tc)], wk=[("msk", tc), ("m2c", tc)],
                       op1=ALU.max, accum_out=t_m2c[tc][:])
                tr.vts(t_cnt[tc][:], t_cnt[tc][:], 2.0, None, ALU.is_ge,
                       rk=[("cnt", tc)], wk=[("cnt", tc)])
                tr.vtt(t_d1[tc][:], t_m1[tc][:], t_m2c[tc][:], ALU.subtract,
                       rk=[("m1", tc), ("m2c", tc)], wk=[("d1", tc)])
                tr.vtt(t_d1[tc][:], t_d1[tc][:], t_cnt[tc][:], ALU.mult,
                       rk=[("d1", tc), ("cnt", tc)], wk=[("d1", tc)])
                tr.vtt(t_m1[tc][:], t_m1[tc][:], t_m2c[tc][:], ALU.add,
                       rk=[("m1", tc), ("m2c", tc)], wk=[("m1", tc)])
                tr.vtt(t_m1[tc][:], t_m1[tc][:], t_d1[tc][:], ALU.add,
                       rk=[("m1", tc), ("d1", tc)], wk=[("m1", tc)])
                tr.transpose(pb[6][0:1, tc * 128:(tc + 1) * 128],
                             t_m1[tc][:], cs["ident"][:],
                             rk=[("m1", tc), ("c", "ident")], wk=PS[6])
                tr.act(t_t2r[0:1, tc * 128:(tc + 1) * 128],
                       pb[6][0:1, tc * 128:(tc + 1) * 128], AF.Copy,
                       rk=[PS[6]], wk=[("t2r", tc)])
            tr.mm(pb[7][:, 0:TT2], cs["ones_row"][:], t_t2r[0:1, :],
                  rk=[("c", "ones_row"), ("t2r", 0), ("t2r", 1), ("t2r", 2),
                      ("t2r", 3)], wk=PS[7])
            tr.vcopy(t_t2b[:], pb[7][:, 0:TT2], rk=[PS[7]], wk=[("t2b",)])
            for mh in range(2):
                for kc in range(16):
                    wb_ = kc % 4
                    slab = t_w2[:, wb_ * 1024:wb_ * 1024 + 1024]
                    tr.load(slab, d_expert[kc * 128:(kc + 1) * 128,
                                           mh * 1024:(mh + 1) * 1024],
                            wk=[("w2", wb_)], ch=f"w2{wb_}")
                    for m in range(8):
                        tr.mm(pb[m][:, 0:TT2], slab[:, m * 128:(m + 1) * 128],
                              t_x1[:, kc * TT2:(kc + 1) * TT2],
                              rk=[("w2", wb_), ("x1", kc)], wk=PS[m],
                              start=(kc == 0), stop=(kc == 15))
                for m in range(8):
                    mc = mh * 8 + m
                    tr.vtt(t_y2[:, mc * TT2:(mc + 1) * TT2], pb[m][:, 0:TT2],
                           t_t2b[:], ALU.mult, rk=[PS[m], ("t2b",)],
                           wk=[("y2", mc)])
            for o in range(8):
                for mc in range(8):
                    bm = mc % 2
                    for kc in range(16):
                        wb_ = kc % 4
                        slab = t_w2[:, wb_ * 1024:wb_ * 1024 + 128]
                        tr.load(slab, d_up[kc * 128:(kc + 1) * 128,
                                           (o * 8 + mc) * 128:
                                           (o * 8 + mc + 1) * 128],
                                wk=[("w2", wb_)], ch=f"w2{wb_}")
                        tr.mm(pb[bm][:, 0:TT2], slab,
                              t_x1[:, kc * TT2:(kc + 1) * TT2],
                              rk=[("w2", wb_), ("x1", kc)], wk=PS[bm],
                              start=(kc == 0), stop=(kc == 15))
                    tr.act(t_sqa[:, 0:TT2], pb[bm][:, 0:TT2], AF.Relu,
                           rk=[PS[bm]], wk=[("sqa", 0)])
                    tr.vtt(t_h2[:, mc * TT2:(mc + 1) * TT2], t_sqa[:, 0:TT2],
                           t_sqa[:, 0:TT2], ALU.mult,
                           rk=[("sqa", 0)], wk=[("h2", mc)])
                for mh in range(2):
                    for kc in range(8):
                        wb_ = kc % 4
                        slab = t_w2[:, wb_ * 1024:wb_ * 1024 + 1024]
                        tr.load(slab, d_down[(o * 8 + kc) * 128:
                                             (o * 8 + kc + 1) * 128,
                                             mh * 1024:(mh + 1) * 1024],
                                wk=[("w2", wb_)], ch=f"w2{wb_}")
                        for m in range(8):
                            tr.mm(pb[m][:, 0:TT2],
                                  slab[:, m * 128:(m + 1) * 128],
                                  t_h2[:, kc * TT2:(kc + 1) * TT2],
                                  rk=[("w2", wb_), ("h2", kc)], wk=PS[m],
                                  start=(kc == 0), stop=(kc == 7))
                    for m in range(8):
                        mc = mh * 8 + m
                        tr.vtt(t_y2[:, mc * TT2:(mc + 1) * TT2],
                               pb[m][:, 0:TT2],
                               t_y2[:, mc * TT2:(mc + 1) * TT2], ALU.add,
                               rk=[PS[m], ("y2", mc)], wk=[("y2", mc)])
            ln_fm(tr, t_y2, t_y2, "y2", "y2", gamma=t_gam, beta=t_bet)
            for mc in range(16):
                tr.vtt(t_y2[:, mc * TT2:(mc + 1) * TT2],
                       t_y2[:, mc * TT2:(mc + 1) * TT2],
                       t_x1[:, mc * TT2:(mc + 1) * TT2], ALU.add,
                       rk=[("y2", mc), ("x1", mc)], wk=[("y2", mc)])
            ln_fm(tr, t_y2, None, "y2", None, store_c2=True)

        def emit_body(body_fn, it, needed=None):
            record = needed is None
            if record:
                needed = {e: set() for e in
                          ("t", "v", "a", "s", "p", "ld", "st")}
            tr = Tracker(nc, sems, needed, record)
            body_fn(tr, it)
            tr.wait_all_dma()
            return needed

        def clear_sems():
            nc.all_engine_barrier()
            for e in ("t", "v", "a", "s", "p"):
                nc.gpsimd.sem_clear(sems[e])
            for ch in DMACH:
                nc.gpsimd.sem_clear(sems["dma:" + ch])
            nc.all_engine_barrier()

        needed = emit_body(lambda tr, it: prologue(tr), {})
        emit_body(lambda tr, it: prologue(tr), {}, needed)
        clear_sems()

        rx = nc.sync.alloc_register("r_xoff")
        rtmp = nc.sync.alloc_register("r_tmp")
        rbase = nc.sync.alloc_register("r_base")
        rc2 = nc.sync.alloc_register("r_c2")
        rtog = nc.sync.alloc_register("r_tog")
        nc.sync.reg_mov(rx, 0)
        nc.all_engine_barrier()
        with nc.Fori(0, nt1, engines=mybir.ALL_ENGINES):
            it = {"ogrow": 0, "ogcol": 0,
                  "regs": {"tmp": rtmp, "rx": rx}}
            needed = emit_body(p1_body, it)
            emit_body(p1_body, it, needed)
            nc.sync.reg_add(rx, rx, TT1)
            clear_sems()

        nc.gpsimd.collective_compute(
            "AllGather", ALU.bypass,
            replica_groups=[[0, 1], [2, 3], [4, 5], [6, 7]],
            ins=[d_oga.ap().opt()], outs=[d_ogf.ap().opt()],
        ).then_inc(sems["cc"])
        nc.gpsimd.wait_ge(sems["cc"], 1)
        if taps:
            nc.gpsimd.dma_start(out=d_tap[:, :], in_=d_ogf[:, :]).then_inc(
                sems["cc"], 16)
            nc.gpsimd.wait_ge(sems["cc"], 17)
        nc.all_engine_barrier()

        nc.sync.reg_load(rbase, t_tb[0:1, 0:1])
        nc.sync.reg_mov(rc2, 0)
        nc.all_engine_barrier()
        rtmp2 = nc.sync.alloc_register("r_tmp2")
        with nc.Fori(0, nt2, engines=mybir.ALL_ENGINES):
            nc.sync.reg_add(rtog, rbase, rc2)
            it = {"regs": {"rc2": rc2, "rtog": rtog, "tmp2": rtmp2}}
            needed = emit_body(p2_body, it)
            emit_body(p2_body, it, needed)
            nc.sync.reg_add(rc2, rc2, TT2)
            clear_sems()
        nc.all_engine_barrier()
    return nc


def make_inmaps(inp):
    x = np.asarray(inp["L0_attn_hidden_states"])
    wqkvz = np.asarray(inp["L0_attn_W_qkvz"])
    wba = np.asarray(inp["L0_attn_W_ba"])
    convw = np.asarray(inp["L0_attn_conv_w"])
    dtb = np.asarray(inp["L0_attn_dt_bias"])
    alog = np.asarray(inp["L0_attn_A_log"])
    nw = np.asarray(inp["L0_attn_norm_weight"])
    wout = np.asarray(inp["L0_attn_W_out"])
    gate = np.asarray(inp["L0_ffn_gate_w"])
    up = np.asarray(inp["L0_ffn_up_w"])
    down = np.asarray(inp["L0_ffn_down_w"])
    expert = np.asarray(inp["L0_ffn_expert_weight"])
    gam = np.asarray(inp["L0_ffn_ln_gamma"])
    bet = np.asarray(inp["L0_ffn_ln_beta"])
    consts = host_constants()
    in_maps = []
    for c in range(8):
        b, g = c // 2, c % 2
        cols = slice(768 * 8 * g, 768 * 8 * (g + 1))
        bcols = [32 * g + 4 * j + r for j in range(8) for r in (0, 1)]
        acols = [32 * g + 4 * j + 2 + r for j in range(8) for r in (0, 1)]
        qrows = np.arange(1024 * g, 1024 * g + 1024)
        krows = np.arange(2048 + 1024 * g, 2048 + 1024 * g + 1024)
        vrows = np.arange(4096 + 2048 * g, 4096 + 2048 * g + 2048)
        convl = convw[np.concatenate([qrows, krows, vrows])]
        convl = np.ascontiguousarray(
            convl.reshape(32, 128, 4).transpose(1, 0, 2).reshape(128, 128))
        m = {
            "x_fm": np.ascontiguousarray(x[b].T),
            "x_own_fm": np.ascontiguousarray(x[b, 4096 * g:4096 * (g + 1)].T),
            "wqkvz": np.ascontiguousarray(wqkvz[:, cols]),
            "wb": np.ascontiguousarray(wba[:, bcols]),
            "wa": np.ascontiguousarray(wba[:, acols]),
            "convw": convl,
            "dtb": np.ascontiguousarray(dtb[16 * g:16 * (g + 1)]).reshape(16, 1),
            "alog": np.ascontiguousarray(alog[16 * g:16 * (g + 1)]).reshape(16, 1),
            "nw": np.ascontiguousarray(nw).reshape(1, 128),
            "wout": np.ascontiguousarray(wout),
            "gate": np.ascontiguousarray(gate),
            "up": np.ascontiguousarray(up),
            "down": np.ascontiguousarray(down),
            "expert": np.ascontiguousarray(expert),
            "gam": np.ascontiguousarray(gam.reshape(16, 128).T),
            "bet": np.ascontiguousarray(bet.reshape(16, 128).T),
        }
        m["tokbase"] = np.array([[4096 * g]], np.int32)
        for k, v in consts.items():
            m["c_" + k] = v
        in_maps.append(m)
    return in_maps


def kernel(**inputs):
    in_maps = make_inmaps(inputs)
    with allow_ici_all_to_all():
        nc = build_program()
        res = run_bass_kernel_spmd(nc, in_maps, core_ids=list(range(8)))
    out = np.empty((B, T, HID), np.float32)
    for c in range(8):
        b, g = c // 2, c % 2
        out[b, 4096 * g:4096 * (g + 1), :] = res.results[c]["out_fm"].T
    return out
